# version 05
# baseline (speedup 1.0000x reference)
import numpy as np
from contextlib import ExitStack

import concourse.bass as bass
import concourse.mybir as mybir
from concourse.bass_utils import run_bass_kernel_spmd

F32 = mybir.dt.float32
BF16 = mybir.dt.bfloat16
AF = mybir.ActivationFunctionType
ALU = mybir.AluOpType
AX = mybir.AxisListType

D = 1024
DC = 8
TT = 512
TC = 4
HM = 4096
HF = 2048
NE = 8
EPS = 1e-5
SLOT_PP = 8192
NSLOT = 2
STOP_AFTER = None


class _StopEmit(Exception):
    pass


def _pieces_manifest():
    man = []
    man.append(("gateA", "gate_w", 0, D, 0, 1024))
    man.append(("gateB", "gate_w", 0, D, 1024, 1024))
    man.append(("state", "state_w", 0, D, 0, 1024))
    man.append(("outw", "out_w", 0, D, 0, 1024))
    for q in range(4):
        man.append((f"mlp1_{q}", "mlp_w1", 0, D, q * 1024, 1024))
    for fh in range(2):
        for kh in range(2):
            man.append((f"mlp2_{kh}{fh}", "mlp_w2", kh * 2048, 2048, fh * 512, 512))
    man.append(("sw1A", "sw1", 0, D, 0, 1024))
    man.append(("sw1B", "sw1", 0, D, 1024, 1024))
    for fh in range(2):
        man.append((f"sw2_{fh}", "sw2", 0, HF, fh * 512, 512))
    for e in range(NE):
        man.append((f"rw1A_{e}", ("rw1", e), 0, D, 0, 1024))
        man.append((f"rw1B_{e}", ("rw1", e), 0, D, 1024, 1024))
        for fh in range(2):
            man.append((f"rw2_{fh}_{e}", ("rw2", e), 0, HF, fh * 512, 512))
    return man


def build(S):
    NT = S // TT
    nc = bass.Bass("TRN2", target_bir_lowering=False)

    dx = nc.dram_tensor("x", [S, D], F32, kind="ExternalInput")
    dy = nc.dram_tensor("y", [S, D], F32, kind="ExternalOutput")
    dw = {}
    for nm, shp in [
        ("gate_w", [D, 2 * D]), ("state_w", [D, D]), ("out_w", [D, D]),
        ("mlp_w1", [D, HM]), ("mlp_w2", [HM, D]),
        ("sw1", [D, HF]), ("sw2", [HF, D]), ("router_w", [D, NE]),
        ("rw1", [NE, D, HF]), ("rw2", [NE, HF, D]),
        ("norm1_w", [1, D]), ("norm1_b", [1, D]), ("norm2_w", [1, D]), ("norm2_b", [1, D]),
        ("gate_b", [1, 2 * D]), ("state_b", [1, D]), ("out_b", [1, D]),
        ("mlp_b1", [1, HM]), ("mlp_b2", [1, D]), ("router_b", [1, NE]),
        ("sb1", [1, HF]), ("sb2", [1, D]), ("rb1", [NE, HF]), ("rb2", [1, NE * D]),
        ("state_b_c", [128, 8]), ("gate_b_c", [128, 16]), ("mlp_b1_c", [128, 32]),
        ("sb1_c", [128, 16]), ("rb1_c", [128, 128]),
        ("n1w_c", [128, 8]), ("n1b_c", [128, 8]), ("n2w_c", [128, 8]), ("n2b_c", [128, 8]),
    ]:
        dw[nm] = nc.dram_tensor(nm, shp, F32, kind="ExternalInput")
    d_id = nc.dram_tensor("ident_f32", [128, 128], F32, kind="ExternalInput")
    d_ones_f = nc.dram_tensor("ones_f", [1, 128], F32, kind="ExternalInput")
    d_ones_b = nc.dram_tensor("ones_b", [1, 128], BF16, kind="ExternalInput")

    man = _pieces_manifest()
    NPIECE = len(man)
    wcache = nc.dram_tensor("wcache", [NPIECE, 128, SLOT_PP], BF16, kind="Internal")

    ctx = ExitStack()
    sb = lambda n, s, d=F32: ctx.enter_context(nc.sbuf_tensor(n, s, d))

    Fx0 = sb("Fx0", [128, 4096])
    Ftn = sb("Ftn", [128, 4096])
    Fa = sb("Fa", [128, 4096])
    Fb = sb("Fb", [128, 4096])
    Fc = sb("Fc", [128, 4096])
    Bxn = sb("Bxn", [128, 4096], BF16)
    Bxg = sb("Bxg", [128, 4096], BF16)
    Bgg = sb("Bgg", [128, 4096], BF16)
    Bys = sb("Bys", [128, 4096], BF16)
    Bh0 = sb("Bh0", [128, 8192], BF16)
    Bh1 = sb("Bh1", [128, 8192], BF16)
    WS = [ctx.enter_context(nc.sbuf_tensor(f"WS{i}", [128, SLOT_PP], BF16)) for i in range(NSLOT)]
    ident = sb("ident", [128, 128])
    ones_f = sb("ones_fs", [1, 128])
    ones_b = sb("ones_bs", [1, 128], BF16)
    rowsb = sb("rowsb", [1, 4096], BF16)
    rb2all = sb("rb2all", [8, 1024], BF16)
    w2kT = sb("w2kT", [8, 512], BF16)
    routw = sb("routw", [128, 64], BF16)
    colsb = sb("colsb", [128, 256])
    CB_STATE, CB_GATE, CB_MLP1, CB_SB1, CB_RB1 = 0, 8, 24, 56, 72
    CB_N1W, CB_N1B, CB_N2W, CB_N2B = 200, 208, 216, 224
    bbar = sb("bbar", [1, 1])
    epsc = sb("epsc", [128, 1])
    wbar = sb("wbar", [128, 8], BF16)
    wbarf = sb("wbarf", [128, 8])
    drA = sb("drA", [1, 512])
    drB = sb("drB", [1, 512])
    drep = sb("drep", [128, 512])
    darep = sb("darep", [128, 512])
    hcol = sb("hcol", [128, 16])
    stt = sb("stt", [128, 48])
    sttb = sb("sttb", [128, 48])
    st2b = sb("st2b", [128, 8])
    smcb = sb("smcb", [128, 24])
    st2 = sb("st2", [128, 8])
    sttc = sb("sttc", [128, 48])
    st2c = sb("st2c", [128, 8])
    smc = sb("smc", [128, 40])
    w2k = sb("w2k", [128, 32])
    pxp = sb("pxp", [128, 32])

    psum = ctx.enter_context(nc.psum_tensor("pall", [128, 4096], F32))
    BK = lambda i: psum[:, 512 * i:512 * (i + 1)]

    sem = {}
    for s in ["sT", "sV", "sS", "qw0", "qw1", "qc0", "qc1", "qx", "qst", "qg0", "qg1", "qb", "qgb"]:
        sem[s] = ctx.enter_context(nc.semaphore(s))
    CNT = {k: 0 for k in sem}
    prog = {k: [] for k in ["T", "V", "S", "Q", "G"]}
    need_drain = {"V": False, "S": False}

    def emit(E, fn, waits=(), inc=None, drain=False):
        ws = [w for w in waits if w is not None]
        if drain and need_drain.get(E):
            prog[E].append(lambda e: e.drain())
        if E in need_drain:
            need_drain[E] = True
        val = None
        if inc is not None:
            iname, iamt = inc
            CNT[iname] += iamt
            val = (iname, CNT[iname])

        def g(e, _ws=tuple(ws), _fn=fn, _inc=inc):
            for s, v in _ws:
                e.wait_ge(sem[s], v)
            inst = _fn(e)
            if _inc is not None:
                inst.then_inc(sem[_inc[0]], _inc[1])
        prog[E].append(g)
        return val

    def T(fn, waits=(), inc=False):
        return emit("T", fn, waits, ("sT", 1) if inc else None)

    def V(fn, waits=(), inc=False):
        return emit("V", fn, waits, ("sV", 1) if inc else None, drain=True)

    def Sc(fn, waits=(), inc=False):
        return emit("S", fn, waits, ("sS", 1) if inc else None, drain=True)

    def Q(fn, waits=(), qsem=None):
        return emit("Q", fn, waits, (qsem, 16) if qsem else None)

    def G(fn, waits=(), qsem=None):
        return emit("G", fn, waits, (qsem, 16) if qsem else None)

    bank_free = [[] for _ in range(8)]

    def use_bank(i):
        evs = bank_free[i]
        bank_free[i] = []
        return evs

    Q(lambda e: e.dma_start(ident[:, :], d_id.ap()), qsem="qb")
    Q(lambda e: e.dma_start(ones_f[:, :], d_ones_f.ap()), qsem="qb")
    Q(lambda e: e.dma_start(ones_b[:, :], d_ones_b.ap()), qsem="qb")
    Q(lambda e: e.dma_start(Ftn[0:1, 0:D], dw["state_b"].ap()), qsem="qb")

    def col_load(dst_c0, src_name, n_chunks):
        Q(lambda e, a=dw[src_name], c0=dst_c0, nch=n_chunks: e.dma_start(
            colsb[:, c0:c0 + nch], a.ap()), qsem="qb")

    col_load(CB_STATE, "state_b_c", 8)
    col_load(CB_GATE, "gate_b_c", 16)
    col_load(CB_MLP1, "mlp_b1_c", 32)
    col_load(CB_SB1, "sb1_c", 16)
    col_load(CB_RB1, "rb1_c", 128)
    col_load(CB_N1W, "n1w_c", 8)
    col_load(CB_N1B, "n1b_c", 8)
    col_load(CB_N2W, "n2w_c", 8)
    col_load(CB_N2B, "n2b_c", 8)
    ev_qb = ("qb", CNT["qb"])

    G(lambda e: e.dma_start(rowsb[:, 0:1024], dw["out_b"].ap()), qsem="qgb")
    G(lambda e: e.dma_start(rowsb[:, 1024:2048], dw["mlp_b2"].ap()), qsem="qgb")
    G(lambda e: e.dma_start(rowsb[:, 2048:3072], dw["sb2"].ap()), qsem="qgb")
    G(lambda e: e.dma_start(rowsb[:, 3072:3072 + NE], dw["router_b"].ap()), qsem="qgb")
    G(lambda e: e.dma_start(rb2all[:, :],
                            dw["rb2"].ap()[0:1, :].rearrange("o (e f) -> (o e) f", f=D)), qsem="qgb")
    G(lambda e: e.dma_start(
        routw[:, :].rearrange("p (c j) -> p c j", j=NE),
        dw["router_w"].ap().rearrange("(c p) j -> p c j", p=128),
        allow_slow_non_contiguous=True), qsem="qgb")
    ev_qgb = ("qgb", CNT["qgb"])

    ev_x = G(lambda e: e.dma_start(
        Fx0[:, :].rearrange("p (tc f) -> p tc f", f=D),
        dx.ap()[0:TT, :].rearrange("(tc p) f -> p tc f", p=128)), qsem="qx")

    V(lambda e: e.memset(epsc[:, :], EPS))
    V(lambda e: e.reduce_sum(bbar[:, :], Ftn[0:1, 0:D], axis=AX.X), waits=[ev_qb])
    ev_bbar_pre = V(lambda e: e.tensor_scalar(bbar[:, :], bbar[:, :], 1.0 / D, None, ALU.mult), inc=True)

    piece_src = {}
    for i, (key, src, k0, K, n0, N) in enumerate(man):
        piece_src[key] = (i, src, k0, K, n0, N)

    slot_free = [[] for _ in range(NSLOT)]
    slot_rr = [0]
    piece_ev = {}
    cache_store_ev = {}

    def load_piece(t, key):
        idx, src, k0, K, n0, N = piece_src[key]
        nch = K // 128
        npp = nch * N
        sl = slot_rr[0]
        slot_rr[0] = (sl + 1) % NSLOT
        dst = WS[sl][:, :npp].rearrange("p (c n) -> p c n", n=N)
        if isinstance(src, tuple):
            srct = dw[src[0]].ap()[src[1]]
        else:
            srct = dw[src].ap()
        srcap = srct[k0:k0 + K, n0:n0 + N].rearrange("(c p) n -> p c n", p=128)
        frees = slot_free[sl]
        slot_free[sl] = []
        if t == 0:
            ev = G(lambda e, d=dst, s=srcap: e.dma_start(d, s),
                   waits=frees, qsem=f"qg{sl}")
            sev = Q(lambda e, i2=idx, npp2=npp, sl2=sl: e.dma_start(
                wcache.ap()[i2][:, :npp2], WS[sl2][:, :npp2]),
                waits=[ev], qsem=f"qc{sl}")
            cache_store_ev[idx] = sev
            slot_free[sl].append(sev)
        else:
            ws = list(frees)
            if t == 1:
                ws.append(cache_store_ev[idx])
            ev = Q(lambda e, d=dst, s=wcache.ap()[idx][:, :npp]: e.dma_start(d, s),
                   waits=ws, qsem=f"qw{sl}")
        piece_ev[key] = (sl, ev)
        return sl, ev

    def consume_piece(key, tev):
        sl, _ = piece_ev[key]
        slot_free[sl].append(tev)

    ev_store = None
    wbar_ready = None

    piece_index = {m[0]: i for i, m in enumerate(man)}

    def ck(name):
        if STOP_AFTER is not None and name == STOP_AFTER:
            raise _StopEmit()

    def emit_A_stats(wait_ev):
        for tc in range(TC):
            for h in range(2):
                V(lambda e, tc=tc, h=h: e.bn_stats(
                    sttb[:, tc * 12 + 6 * h: tc * 12 + 6 * h + 6],
                    Fx0[:, tc * 1024 + 512 * h: tc * 1024 + 512 * (h + 1)]),
                  waits=[wait_ev] if (tc == 0 and h == 0) else ())
            V(lambda e, tc=tc: e.bn_aggr(st2b[:, tc * 2:tc * 2 + 2], sttb[:, tc * 12:tc * 12 + 12]))
        ev_st = V(lambda e: e.tensor_copy(smcb[:, 0:4], st2b[:, 1:8:2]), inc=True)
        Sc(lambda e: e.activation(smcb[:, 4:8], smcb[:, 0:4], AF.Ln, bias=epsc[:, 0:1]), waits=[ev_st])
        ev_r0 = Sc(lambda e: e.activation(smcb[:, 8:12], smcb[:, 4:8], AF.Exp, scale=-0.5), inc=True)
        V(lambda e: e.tensor_tensor(smcb[:, 12:16], smcb[:, 8:12], smcb[:, 8:12], ALU.mult), waits=[ev_r0])
        ev_v1 = V(lambda e: e.tensor_tensor(smcb[:, 12:16], smcb[:, 12:16], smcb[:, 0:4], ALU.mult), inc=True)
        Sc(lambda e: e.activation(smcb[:, 16:20], smcb[:, 12:16], AF.Ln, bias=epsc[:, 0:1]), waits=[ev_v1])
        ev_r1 = Sc(lambda e: e.activation(smcb[:, 16:20], smcb[:, 16:20], AF.Exp, scale=-0.5), inc=True)
        return V(lambda e: e.tensor_tensor(smcb[:, 20:24], smcb[:, 8:12], smcb[:, 16:20], ALU.mult),
                 waits=[ev_r1], inc=True)

    def emit_Anorm(waits):
        ev = None
        for tc in range(TC):
            ev = V(lambda e, tc=tc: e.tensor_scalar(
                Ftn[:, tc * 1024:(tc + 1) * 1024], Fx0[:, tc * 1024:(tc + 1) * 1024],
                st2b[:, 2 * tc:2 * tc + 1], smcb[:, 20 + tc:21 + tc],
                ALU.subtract, ALU.mult), waits=waits if tc == 0 else (), inc=(tc == TC - 1))
        return ev

    def emit_AB(ev_tnorm, first):
        ev_trs = {}
        for fc in range(DC):
            wt = use_bank(fc) + ([ev_tnorm] if fc == 0 else [])
            if first and fc == 0:
                wt += [ev_qb, ev_qgb]
            for tcj in range(TC):
                ev_trs[fc] = T(lambda e, fc=fc, tcj=tcj: e.transpose(
                    BK(fc)[:, tcj * 128:(tcj + 1) * 128],
                    Ftn[:, tcj * 1024 + fc * 128: tcj * 1024 + (fc + 1) * 128],
                    ident[:, :]), waits=wt if tcj == 0 else (), inc=(tcj == TC - 1))
        ev_xnf = {}
        ev_xnb = None
        for fc in range(DC):
            ev_xnf[fc] = Sc(lambda e, fc=fc: e.activation(
                Ftn[:, fc * 512:(fc + 1) * 512], BK(fc), AF.Identity,
                scale=colsb[:, CB_N1W + fc:CB_N1W + fc + 1],
                bias=colsb[:, CB_N1B + fc:CB_N1B + fc + 1]),
                waits=[ev_trs[DC - 1]] if fc == 0 else (), inc=True)
            bank_free[fc].append(ev_xnf[fc])
            ev_xnb = V(lambda e, fc=fc: e.tensor_copy(
                Bxn[:, fc * 512:(fc + 1) * 512], Ftn[:, fc * 512:(fc + 1) * 512]),
                waits=[ev_xnf[fc]], inc=True)
        return ev_xnb, ev_xnf

    emit_A_stats(ev_x)
    AB_stash = None
    ev_fa_free = None

    try:
      for t in range(NT):
        par = t % 2
        next_pi = [0]

        def ensure(key, ahead=1, _t=t, _next=None):
            target = min(piece_index[key] + ahead, NPIECE - 1)
            while next_pi[0] <= target:
                load_piece(t, man[next_pi[0]][0])
                next_pi[0] += 1

        if t == 0:
            ev_tnorm0 = emit_Anorm([])
            ev_xnb, ev_xnf = emit_AB(ev_tnorm0, first=True)
        else:
            ev_xnb, ev_xnf = AB_stash

        ck("B")
        ev_xg = {}
        ev_gg = {}
        for half, key in enumerate(["gateA", "gateB"]):
            ensure(key)
            slg, evg = piece_ev[key]
            ev_last_mm = None
            for fo in range(8):
                bk = fo % 4
                wt = use_bank(bk) + ([evg, ev_xnb] if fo in (0, 1, 2, 3) else [])
                for kc in range(DC):
                    ev_last_mm = T(lambda e, bk=bk, slg=slg, kc=kc, fo=fo: e.matmul(
                        BK(bk), WS[slg][:, kc * 1024 + fo * 128: kc * 1024 + (fo + 1) * 128],
                        Bxn[:, kc * 512:(kc + 1) * 512],
                        start=(kc == 0), stop=(kc == DC - 1)),
                        waits=wt if kc == 0 else (), inc=(kc == DC - 1))
                if half == 0:
                    evs = Sc(lambda e, bk=bk, fo=fo: e.activation(
                        BK(bk), BK(bk), AF.Silu,
                        bias=colsb[:, CB_GATE + fo:CB_GATE + fo + 1]),
                        waits=[ev_last_mm], inc=True)
                    evv = V(lambda e, bk=bk, fo=fo: e.tensor_tensor(
                        Bxg[:, fo * 512:(fo + 1) * 512], BK(bk),
                        Ftn[:, fo * 512:(fo + 1) * 512], ALU.mult),
                        waits=[evs], inc=True)
                    ev_xg[fo] = evv
                    bank_free[bk].append(evv)
                else:
                    evs = Sc(lambda e, bk=bk, fo=fo: e.activation(
                        Bgg[:, fo * 512:(fo + 1) * 512], BK(bk), AF.Silu,
                        bias=colsb[:, CB_GATE + 8 + fo:CB_GATE + 8 + fo + 1]),
                        waits=[ev_last_mm], inc=True)
                    ev_gg[fo] = evs
                    bank_free[bk].append(evs)
            consume_piece(key, ev_last_mm)

        ck("C")
        ensure("state")
        sls, evsl = piece_ev["state"]
        ev_u = {}
        ev_bm = {}
        ev_last_state_mm = None
        for fo in range(8):
            for which in range(2):
                bk = (2 * fo + which) % 4
                rhs_buf = Bxg if which == 0 else Bgg
                rdy = ev_xg[7] if which == 0 else ev_gg[7]
                wt = use_bank(bk) + ([evsl, rdy] if fo == 0 else [])
                for kc in range(DC):
                    ev_last_state_mm = T(lambda e, bk=bk, sls=sls, kc=kc, fo=fo, rb=rhs_buf: e.matmul(
                        BK(bk), WS[sls][:, kc * 1024 + fo * 128: kc * 1024 + (fo + 1) * 128],
                        rb[:, kc * 512:(kc + 1) * 512],
                        start=(kc == 0), stop=(kc == DC - 1)),
                        waits=wt if kc == 0 else (), inc=(kc == DC - 1))
                dstb = Fa if which == 0 else Fb
                ew = [ev_last_state_mm]
                if which == 0 and fo == 0:
                    ew.append(ev_fa_free)
                evs = Sc(lambda e, bk=bk, fo=fo, dstb=dstb: e.activation(
                    dstb[:, fo * 512:(fo + 1) * 512], BK(bk), AF.Identity,
                    bias=colsb[:, CB_STATE + fo:CB_STATE + fo + 1]),
                    waits=ew, inc=True)
                bank_free[bk].append(evs)
                if which == 0:
                    ev_u[fo] = evs
                else:
                    ev_bm[fo] = evs
        consume_piece("state", ev_last_state_mm)

        ck("E")
        if t == 0:
            slst = sls
            for kc in range(DC):
                V(lambda e, kc=kc, s=slst: e.reduce_sum(
                    wbarf[:, kc:kc + 1], WS[s][:, kc * 1024:(kc + 1) * 1024], axis=AX.X),
                  waits=[evsl] if kc == 0 else ())
            V(lambda e: e.tensor_scalar(wbarf[:, :], wbarf[:, :], 1.0 / D, None, ALU.mult))
            wbar_ready = V(lambda e: e.tensor_copy(wbar[:, :], wbarf[:, :]), inc=True)
            slot_free[sls].append(wbar_ready)

        wt = use_bank(6) + [ev_xnb, wbar_ready]
        ev_dmm = None
        for kc in range(DC):
            ev_dmm = T(lambda e, kc=kc: e.matmul(
                BK(6)[0:1, :], wbar[:, kc:kc + 1], Bxn[:, kc * 512:(kc + 1) * 512],
                start=(kc == 0), stop=(kc == DC - 1)),
                waits=wt if kc == 0 else (), inc=(kc == DC - 1))
        Sc(lambda e: e.activation(drA[0:1, :], BK(6)[0:1, :], AF.Exp, bias=bbar[0:1, 0:1]),
           waits=[ev_dmm, ev_bbar_pre])
        ev_dsp = Sc(lambda e: e.activation(drB[0:1, :], drA[0:1, :], AF.Ln, bias=1.0), inc=True)
        ev_dA = Sc(lambda e: e.activation(drA[0:1, :], drB[0:1, :], AF.Exp, scale=-1.0), inc=True)
        bank_free[6].append(ev_dsp)
        wt6 = use_bank(6)
        ev_rp1 = T(lambda e: e.matmul(BK(6), ones_f[0:1, :], drB[0:1, :], start=True, stop=True),
                   waits=wt6 + [ev_dsp], inc=True)
        wt7 = use_bank(7)
        ev_rp2 = T(lambda e: e.matmul(BK(7), ones_f[0:1, :], drA[0:1, :], start=True, stop=True),
                   waits=wt7 + [ev_dA], inc=True)
        ev_drep = V(lambda e: e.tensor_copy(drep[:, :], BK(6)), waits=[ev_rp1], inc=True)
        ev_darep = V(lambda e: e.tensor_copy(darep[:, :], BK(7)), waits=[ev_rp2], inc=True)
        bank_free[6].append(ev_drep)
        bank_free[7].append(ev_darep)

        ck("D")
        ev_dbu = {}
        for fo in range(8):
            wdep = [ev_u[fo], ev_bm[fo], ev_drep]
            if ev_store is not None and fo == 0:
                wdep.append(ev_store)
            V(lambda e, fo=fo: e.tensor_tensor(
                Fc[:, fo * 512:(fo + 1) * 512], Fa[:, fo * 512:(fo + 1) * 512],
                Fb[:, fo * 512:(fo + 1) * 512], ALU.mult), waits=wdep)
            ev_dbu[fo] = V(lambda e, fo=fo: e.tensor_tensor(
                Fc[:, fo * 512:(fo + 1) * 512], Fc[:, fo * 512:(fo + 1) * 512],
                drep[:, :], ALU.mult), inc=True)

        ev_scan = {}
        for fo in range(8):
            if t == 0:
                init = 0.0
            else:
                init = hcol[:, (1 - par) * 8 + fo:(1 - par) * 8 + fo + 1]
            ev_scan[fo] = V(lambda e, fo=fo, init=init: e.tensor_tensor_scan(
                Bys[:, fo * 512:(fo + 1) * 512], darep[:, :],
                Fc[:, fo * 512:(fo + 1) * 512], init, ALU.mult, ALU.add),
                waits=[ev_dbu[fo], ev_darep], inc=True)
            V(lambda e, fo=fo, par=par: e.tensor_copy(
                hcol[:, par * 8 + fo:par * 8 + fo + 1],
                Bys[:, fo * 512 + 511:fo * 512 + 512]))

        ck("F")
        ensure("outw")
        slo, evo = piece_ev["outw"]
        ev_mid = {}
        ev_last_out_mm = None
        ev_xin = None
        for tc in range(TC):
            ev_xin = V(lambda e, tc=tc: e.tensor_scalar(
                Ftn[:, tc * 1024:(tc + 1) * 1024], Fx0[:, tc * 1024:(tc + 1) * 1024],
                st2b[:, 2 * tc:2 * tc + 1], smcb[:, 8 + tc:9 + tc],
                ALU.subtract, ALU.mult),
                waits=[ev_xnf[DC - 1]] if tc == 0 else (), inc=(tc == TC - 1))
        for tc in range(TC):
            for nh in range(2):
                bk = 2 + ((2 * tc + nh) % 4)
                wt = use_bank(bk) + ([evo, ev_scan[7]] if tc == 0 else [])
                for kc in range(DC):
                    T(lambda e, bk=bk, slo=slo, kc=kc, tc=tc, nh=nh: e.matmul(
                        BK(bk), Bys[:, kc * 512 + tc * 128: kc * 512 + (tc + 1) * 128],
                        WS[slo][:, kc * 1024 + nh * 512: kc * 1024 + (nh + 1) * 512],
                        start=(kc == 0), stop=False),
                        waits=wt if kc == 0 else ())
                ev_last_out_mm = T(lambda e, bk=bk, nh=nh: e.matmul(
                    BK(bk), ones_b[0:1, :], rowsb[0:1, nh * 512:(nh + 1) * 512],
                    start=False, stop=True), inc=True)
                evv = V(lambda e, bk=bk, tc=tc, nh=nh: e.tensor_tensor(
                    Fc[:, tc * 1024 + nh * 512: tc * 1024 + (nh + 1) * 512], BK(bk),
                    Ftn[:, tc * 1024 + nh * 512: tc * 1024 + (nh + 1) * 512], ALU.add),
                    waits=[ev_last_out_mm, ev_xin] if (tc == 0 and nh == 0) else [ev_last_out_mm],
                    inc=True)
                ev_mid[(tc, nh)] = evv
                bank_free[bk].append(evv)
                V(lambda e, tc=tc, nh=nh: e.bn_stats(
                    stt[:, tc * 12 + 6 * nh: tc * 12 + 6 * nh + 6],
                    Fc[:, tc * 1024 + 512 * nh: tc * 1024 + 512 * (nh + 1)]))
                if nh == 1:
                    V(lambda e, tc=tc: e.bn_aggr(st2[:, tc * 2:tc * 2 + 2],
                                                 stt[:, tc * 12:tc * 12 + 12]))
        consume_piece("outw", ev_last_out_mm)
        ev_mid_all = ev_mid[(TC - 1, 1)]

        ck("G")
        ev_st = V(lambda e: e.engine_nop(), inc=True)
        Sc(lambda e: e.activation(smc[:, 4:8], st2[:, 1:8:2], AF.Ln, bias=epsc[:, 0:1]), waits=[ev_st])
        ev_r2 = Sc(lambda e: e.activation(smc[:, 8:12], smc[:, 4:8], AF.Exp, scale=-0.5), inc=True)
        ev_tnorm2 = None
        for tc in range(TC):
            ev_tnorm2 = V(lambda e, tc=tc: e.tensor_scalar(
                Ftn[:, tc * 1024:(tc + 1) * 1024], Fc[:, tc * 1024:(tc + 1) * 1024],
                st2[:, 2 * tc:2 * tc + 1], smc[:, 8 + tc:9 + tc],
                ALU.subtract, ALU.mult), waits=[ev_r2] if tc == 0 else (), inc=(tc == TC - 1))
        ev_x2 = {}
        for fc in range(DC):
            bk = fc
            wt = use_bank(bk) + ([ev_tnorm2] if fc in (0, 1) else [])
            for tcj in range(TC):
                ev_tr = T(lambda e, bk=bk, tcj=tcj, fc=fc: e.transpose(
                    BK(bk)[:, tcj * 128:(tcj + 1) * 128],
                    Ftn[:, tcj * 1024 + fc * 128: tcj * 1024 + (fc + 1) * 128],
                    ident[:, :]), waits=wt if tcj == 0 else (), inc=(tcj == TC - 1))
            ev_x2[fc] = Sc(lambda e, bk=bk, fc=fc: e.activation(
                Bxn[:, fc * 512:(fc + 1) * 512], BK(bk), AF.Identity,
                scale=colsb[:, CB_N2W + fc:CB_N2W + fc + 1],
                bias=colsb[:, CB_N2B + fc:CB_N2B + fc + 1]), waits=[ev_tr], inc=True)
            bank_free[bk].append(ev_x2[fc])
        ev_x2_all = ev_x2[DC - 1]

        ck("H")
        ev_h1 = {}
        for q in range(4):
            key = f"mlp1_{q}"
            ensure(key)
            slq, evq = piece_ev[key]
            ev_last = None
            for fl in range(8):
                hc = q * 8 + fl
                bk = hc % 4
                wt = use_bank(bk) + ([evq, ev_x2_all] if fl in (0, 1, 2, 3) else [])
                for kc in range(DC):
                    ev_last = T(lambda e, bk=bk, slq=slq, kc=kc, fl=fl: e.matmul(
                        BK(bk), WS[slq][:, kc * 1024 + fl * 128: kc * 1024 + (fl + 1) * 128],
                        Bxn[:, kc * 512:(kc + 1) * 512],
                        start=(kc == 0), stop=(kc == DC - 1)),
                        waits=wt if kc == 0 else (), inc=(kc == DC - 1))
                dsth = Bh0 if hc < 16 else Bh1
                evs = Sc(lambda e, bk=bk, hc=hc, dsth=dsth: e.activation(
                    dsth[:, (hc % 16) * 512:((hc % 16) + 1) * 512], BK(bk), AF.Gelu,
                    bias=colsb[:, CB_MLP1 + hc:CB_MLP1 + hc + 1]),
                    waits=[ev_last], inc=True)
                ev_h1[hc] = evs
                bank_free[bk].append(evs)
            consume_piece(key, ev_last)

        ck("I")
        ev_bout = {}
        for fh in range(2):
            for kh in range(2):
                key = f"mlp2_{kh}{fh}"
                ensure(key)
                slm, evm = piece_ev[key]
                hbuf = Bh0 if kh == 0 else Bh1
                ev_pl = None
                for tc in range(TC):
                    bk = 2 + tc
                    if kh == 0:
                        wt = use_bank(bk) + [evm, ev_h1[15]]
                    else:
                        wt = [evm, ev_h1[31]]
                    for kl in range(16):
                        ev_pl = T(lambda e, bk=bk, slm=slm, kl=kl, tc=tc, hbuf=hbuf, kh=kh: e.matmul(
                            BK(bk), hbuf[:, kl * 512 + tc * 128: kl * 512 + (tc + 1) * 128],
                            WS[slm][:, kl * 512:(kl + 1) * 512],
                            start=(kh == 0 and kl == 0), stop=False),
                            waits=wt if kl == 0 else (),
                            inc=(tc == TC - 1 and kl == 15))
                consume_piece(key, ev_pl)
            evb = None
            for tc in range(TC):
                bk = 2 + tc
                evb = T(lambda e, bk=bk, fh=fh: e.matmul(
                    BK(bk), ones_b[0:1, :], rowsb[0:1, 1024 + fh * 512:1024 + (fh + 1) * 512],
                    start=False, stop=True), inc=True)
                evv = V(lambda e, bk=bk, tc=tc, fh=fh: e.tensor_tensor(
                    Fc[:, tc * 1024 + fh * 512: tc * 1024 + (fh + 1) * 512], BK(bk),
                    Fc[:, tc * 1024 + fh * 512: tc * 1024 + (fh + 1) * 512], ALU.add),
                    waits=[evb], inc=True)
                ev_bout[(tc, fh)] = evv
                bank_free[bk].append(evv)
                V(lambda e, tc=tc, fh=fh: e.bn_stats(
                    sttc[:, tc * 12 + 6 * fh: tc * 12 + 6 * fh + 6],
                    Fc[:, tc * 1024 + 512 * fh: tc * 1024 + 512 * (fh + 1)]))
                if fh == 1:
                    V(lambda e, tc=tc: e.bn_aggr(st2c[:, tc * 2:tc * 2 + 2],
                                                 sttc[:, tc * 12:tc * 12 + 12]))

        ck("J")
        ev_st = V(lambda e: e.engine_nop(), inc=True)
        Sc(lambda e: e.activation(smc[:, 4:8], st2c[:, 1:8:2], AF.Ln, bias=epsc[:, 0:1]), waits=[ev_st])
        ev_r3 = Sc(lambda e: e.activation(smc[:, 8:12], smc[:, 4:8], AF.Exp, scale=-0.5), inc=True)
        ev_x1 = None
        for tc in range(TC):
            V(lambda e, tc=tc: e.tensor_scalar(
                Ftn[:, tc * 1024:(tc + 1) * 1024], Fc[:, tc * 1024:(tc + 1) * 1024],
                st2c[:, 2 * tc:2 * tc + 1], smc[:, 8 + tc:9 + tc],
                ALU.subtract, ALU.mult), waits=[ev_r3] if tc == 0 else ())
            ev_x1 = V(lambda e, tc=tc: e.tensor_tensor(
                Fa[:, tc * 1024:(tc + 1) * 1024], Ftn[:, tc * 1024:(tc + 1) * 1024],
                Fx0[:, tc * 1024:(tc + 1) * 1024], ALU.add), inc=(tc == TC - 1))
            for h in range(2):
                V(lambda e, tc=tc, h=h: e.bn_stats(
                    stt[:, tc * 12 + 6 * h: tc * 12 + 6 * h + 6],
                    Fa[:, tc * 1024 + 512 * h: tc * 1024 + 512 * (h + 1)]))
            V(lambda e, tc=tc: e.bn_aggr(st2[:, tc * 2:tc * 2 + 2], stt[:, tc * 12:tc * 12 + 12]))
        if t + 1 < NT:
            ev_x = G(lambda e, tt=t + 1: e.dma_start(
                Fx0[:, :].rearrange("p (tc f) -> p tc f", f=D),
                dx.ap()[tt * TT:(tt + 1) * TT, :].rearrange("(tc p) f -> p tc f", p=128)),
                waits=[ev_x1], qsem="qx")
            emit_A_stats(ev_x)

        ck("K")
        ev_st = V(lambda e: e.engine_nop(), inc=True)
        Sc(lambda e: e.activation(smc[:, 4:8], st2[:, 1:8:2], AF.Ln, bias=epsc[:, 0:1]), waits=[ev_st])
        ev_r4 = Sc(lambda e: e.activation(smc[:, 8:12], smc[:, 4:8], AF.Exp, scale=-0.5), inc=True)
        ev_tnorm4 = None
        for tc in range(TC):
            ev_tnorm4 = V(lambda e, tc=tc: e.tensor_scalar(
                Ftn[:, tc * 1024:(tc + 1) * 1024], Fa[:, tc * 1024:(tc + 1) * 1024],
                st2[:, 2 * tc:2 * tc + 1], smc[:, 8 + tc:9 + tc],
                ALU.subtract, ALU.mult), waits=[ev_r4] if tc == 0 else (), inc=(tc == TC - 1))
        ev_fin = {}
        for fc in range(DC):
            bk = fc
            wt = use_bank(bk) + ([ev_tnorm4] if fc in (0, 1) else [])
            for tcj in range(TC):
                ev_tr = T(lambda e, bk=bk, tcj=tcj, fc=fc: e.transpose(
                    BK(bk)[:, tcj * 128:(tcj + 1) * 128],
                    Ftn[:, tcj * 1024 + fc * 128: tcj * 1024 + (fc + 1) * 128],
                    ident[:, :]), waits=wt if tcj == 0 else (), inc=(tcj == TC - 1))
            ev_fin[fc] = Sc(lambda e, bk=bk, fc=fc: e.activation(
                Bgg[:, fc * 512:(fc + 1) * 512], BK(bk), AF.Identity), waits=[ev_tr], inc=True)
            bank_free[bk].append(ev_fin[fc])
        ev_fin_all = ev_fin[DC - 1]
        ev_trL = ev_tr
        ev_tnorm_next = emit_Anorm([ev_trL]) if t + 1 < NT else None

        ck("L")
        ev_rt = {}
        ev_w2k = None
        for tc in range(TC):
            bk = 6 + (tc % 2)
            wt = use_bank(bk)
            if tc == 0:
                wt += [ev_fin_all, ev_qgb]
            for kc in range(DC):
                T(lambda e, tc=tc, kc=kc, bk=bk: e.matmul(
                    BK(bk)[:, tc * 8:tc * 8 + 8],
                    Bgg[:, kc * 512 + tc * 128: kc * 512 + (tc + 1) * 128],
                    routw[:, kc * 8:(kc + 1) * 8],
                    start=(kc == 0), stop=False),
                    waits=wt if kc == 0 else ())
            ev_rt[tc] = T(lambda e, tc=tc, bk=bk: e.matmul(
                BK(bk)[:, tc * 8:tc * 8 + 8], ones_b[0:1, :], rowsb[0:1, 3072:3072 + 8],
                start=False, stop=True), inc=True)
            c0 = tc * 8
            V(lambda e, tc=tc, c0=c0, bk=bk: e.reduce_max(smc[:, 24:25], BK(bk)[:, c0:c0 + 8], axis=AX.X),
              waits=[ev_rt[tc]])
            ev_ngm = V(lambda e: e.tensor_scalar(smc[:, 25:26], smc[:, 24:25], -1.0, None, ALU.mult),
                       inc=True)
            evsx = Sc(lambda e, c0=c0, bk=bk: e.activation(pxp[:, c0:c0 + 8], BK(bk)[:, c0:c0 + 8],
                                                           AF.Exp, bias=smc[:, 25:26]),
                      waits=[ev_ngm], inc=True)
            bank_free[bk].append(evsx)
            V(lambda e, c0=c0: e.reduce_sum(smc[:, 26:27], pxp[:, c0:c0 + 8], axis=AX.X),
              waits=[evsx])
            V(lambda e: e.reciprocal(smc[:, 27:28], smc[:, 26:27]))
            V(lambda e, c0=c0: e.tensor_scalar(pxp[:, c0:c0 + 8], pxp[:, c0:c0 + 8],
                                               smc[:, 27:28], None, ALU.mult))
            V(lambda e, c0=c0: e.reduce_max(smc[:, 24:25], pxp[:, c0:c0 + 8], axis=AX.X))
            V(lambda e, c0=c0: e.tensor_scalar(w2k[:, c0:c0 + 8], pxp[:, c0:c0 + 8],
                                               smc[:, 24:25], None, ALU.is_ge))
            V(lambda e, c0=c0: e.tensor_scalar(w2k[:, c0:c0 + 8], w2k[:, c0:c0 + 8],
                                               -1e30, None, ALU.mult))
            V(lambda e, c0=c0: e.tensor_tensor(w2k[:, c0:c0 + 8], w2k[:, c0:c0 + 8],
                                               pxp[:, c0:c0 + 8], ALU.add))
            V(lambda e, c0=c0: e.reduce_max(smc[:, 24:25], w2k[:, c0:c0 + 8], axis=AX.X))
            V(lambda e, c0=c0: e.tensor_scalar(w2k[:, c0:c0 + 8], pxp[:, c0:c0 + 8],
                                               smc[:, 24:25], None, ALU.is_ge))
            ev_w2k = V(lambda e, c0=c0: e.tensor_tensor(w2k[:, c0:c0 + 8], w2k[:, c0:c0 + 8],
                                                        pxp[:, c0:c0 + 8], ALU.mult),
                       inc=(tc == TC - 1))

        wt6 = use_bank(6)
        ev_trw = None
        for tc in range(TC):
            ev_trw = T(lambda e, tc=tc: e.transpose(
                BK(6)[0:8, tc * 128:(tc + 1) * 128], w2k[:, tc * 8:tc * 8 + 8],
                ident[:, :]), waits=(wt6 + [ev_w2k]) if tc == 0 else (), inc=(tc == TC - 1))
        ev_w2kT = Sc(lambda e: e.activation(w2kT[0:8, :], BK(6)[0:8, :], AF.Identity),
                     waits=[ev_trw], inc=True)
        bank_free[6].append(ev_w2kT)

        ck("M")
        ev_hsh = {}
        for half, key in enumerate(["sw1A", "sw1B"]):
            ensure(key)
            slg, evg = piece_ev[key]
            ev_last = None
            for fl in range(8):
                hc = half * 8 + fl
                bk = hc % 4
                wt = use_bank(bk) + ([evg, ev_fin_all] if fl in (0, 1, 2, 3) else [])
                for kc in range(DC):
                    ev_last = T(lambda e, bk=bk, slg=slg, kc=kc, fl=fl: e.matmul(
                        BK(bk), WS[slg][:, kc * 1024 + fl * 128: kc * 1024 + (fl + 1) * 128],
                        Bgg[:, kc * 512:(kc + 1) * 512],
                        start=(kc == 0), stop=(kc == DC - 1)),
                        waits=wt if kc == 0 else (), inc=(kc == DC - 1))
                evs = Sc(lambda e, bk=bk, hc=hc: e.activation(
                    Bh0[:, hc * 512:(hc + 1) * 512], BK(bk), AF.Gelu,
                    bias=colsb[:, CB_SB1 + hc:CB_SB1 + hc + 1]),
                    waits=[ev_last], inc=True)
                ev_hsh[hc] = evs
                bank_free[bk].append(evs)
            consume_piece(key, ev_last)
        ck("N1")
        ev_mix = {}
        for fh in range(2):
            key = f"sw2_{fh}"
            ensure(key)
            slm, evm = piece_ev[key]
            ev_pl = None
            for tc in range(TC):
                bk = 2 + tc
                wt = use_bank(bk) + ([evm, ev_hsh[15]] if tc == 0 else [])
                for kl in range(16):
                    ev_pl = T(lambda e, bk=bk, slm=slm, kl=kl, tc=tc: e.matmul(
                        BK(bk), Bh0[:, kl * 512 + tc * 128: kl * 512 + (tc + 1) * 128],
                        WS[slm][:, kl * 512:(kl + 1) * 512],
                        start=(kl == 0), stop=False),
                        waits=wt if kl == 0 else (),
                        inc=(tc == TC - 1 and kl == 15))
                T(lambda e, bk=bk, tc=tc, fh=fh: e.matmul(
                    BK(bk), w2kT[0:8, tc * 128:(tc + 1) * 128],
                    rb2all[0:8, fh * 512:(fh + 1) * 512],
                    start=False, stop=False),
                    waits=[ev_w2kT] if (tc == 0 and fh == 0) else ())
                evb = T(lambda e, bk=bk, fh=fh: e.matmul(
                    BK(bk), ones_b[0:1, :], rowsb[0:1, 2048 + fh * 512:2048 + (fh + 1) * 512],
                    start=False, stop=True), inc=True)
                evv = V(lambda e, bk=bk, tc=tc, fh=fh: e.tensor_copy(
                    Fb[:, tc * 1024 + fh * 512: tc * 1024 + (fh + 1) * 512], BK(bk)),
                    waits=[evb], inc=True)
                ev_mix[(tc, fh)] = evv
                bank_free[bk].append(evv)
            consume_piece(key, ev_pl)

        ck("N")
        for e_i in range(NE):
            hbuf = Bh1 if (e_i % 2 == 0) else Bh0
            ev_he = {}
            for half, key in enumerate([f"rw1A_{e_i}", f"rw1B_{e_i}"]):
                ensure(key)
                slg, evg = piece_ev[key]
                ev_last = None
                for fl in range(8):
                    hc = half * 8 + fl
                    bk = (0, 1, 6, 7)[hc % 4]
                    wt = use_bank(bk) + ([evg] if fl in (0, 1, 2, 3) else [])
                    for kc in range(DC):
                        ev_last = T(lambda e, bk=bk, slg=slg, kc=kc, fl=fl: e.matmul(
                            BK(bk), WS[slg][:, kc * 1024 + fl * 128: kc * 1024 + (fl + 1) * 128],
                            Bgg[:, kc * 512:(kc + 1) * 512],
                            start=(kc == 0), stop=(kc == DC - 1)),
                            waits=wt if kc == 0 else (), inc=(kc == DC - 1))
                    evs = Sc(lambda e, bk=bk, hc=hc, hbuf=hbuf, e_i=e_i: e.activation(
                        hbuf[:, hc * 512:(hc + 1) * 512], BK(bk), AF.Gelu,
                        bias=colsb[:, CB_RB1 + e_i * 16 + hc:CB_RB1 + e_i * 16 + hc + 1]),
                        waits=[ev_last], inc=True)
                    ev_he[hc] = evs
                    bank_free[bk].append(evs)
                consume_piece(key, ev_last)
            for fh in range(2):
                key = f"rw2_{fh}_{e_i}"
                ensure(key)
                slm, evm = piece_ev[key]
                ev_pl = None
                for tc in range(TC):
                    bk = 2 + tc
                    wt = use_bank(bk) + ([evm, ev_he[15]] if tc == 0 else [])
                    evb = None
                    for kl in range(16):
                        evb = ev_pl = T(lambda e, bk=bk, slm=slm, kl=kl, tc=tc, hbuf=hbuf: e.matmul(
                            BK(bk), hbuf[:, kl * 512 + tc * 128: kl * 512 + (tc + 1) * 128],
                            WS[slm][:, kl * 512:(kl + 1) * 512],
                            start=(kl == 0), stop=(kl == 15)),
                            waits=wt if kl == 0 else (),
                            inc=(kl == 15))
                    mixw = [evb, ev_w2k] + ([ev_mix[(tc, fh)]] if e_i == 0 else [])
                    evv = V(lambda e, bk=bk, tc=tc, fh=fh, e_i=e_i: e.scalar_tensor_tensor(
                        Fb[:, tc * 1024 + fh * 512: tc * 1024 + (fh + 1) * 512], BK(bk),
                        w2k[:, tc * 8 + e_i:tc * 8 + e_i + 1],
                        Fb[:, tc * 1024 + fh * 512: tc * 1024 + (fh + 1) * 512],
                        ALU.mult, ALU.add),
                        waits=mixw, inc=True)
                    ev_mix[(tc, fh)] = evv
                    bank_free[bk].append(evv)
                    if e_i == NE - 1:
                        V(lambda e, tc=tc, fh=fh: e.bn_stats(
                            sttc[:, tc * 12 + 6 * fh: tc * 12 + 6 * fh + 6],
                            Fb[:, tc * 1024 + 512 * fh: tc * 1024 + 512 * (fh + 1)]))
                        if fh == 1:
                            V(lambda e, tc=tc: e.bn_aggr(st2c[:, tc * 2:tc * 2 + 2],
                                                         sttc[:, tc * 12:tc * 12 + 12]))
                consume_piece(key, ev_pl)

        ck("O")
        ev_st = V(lambda e: e.engine_nop(), inc=True)
        Sc(lambda e: e.activation(smc[:, 4:8], st2c[:, 1:8:2], AF.Ln, bias=epsc[:, 0:1]), waits=[ev_st])
        ev_r5 = Sc(lambda e: e.activation(smc[:, 8:12], smc[:, 4:8], AF.Exp, scale=-0.5), inc=True)
        ev_out = None
        for tc in range(TC):
            V(lambda e, tc=tc: e.tensor_scalar(
                Fc[:, tc * 1024:(tc + 1) * 1024], Fb[:, tc * 1024:(tc + 1) * 1024],
                st2c[:, 2 * tc:2 * tc + 1], smc[:, 8 + tc:9 + tc],
                ALU.subtract, ALU.mult), waits=[ev_r5] if tc == 0 else ())
            ev_out = V(lambda e, tc=tc: e.tensor_tensor(
                Fc[:, tc * 1024:(tc + 1) * 1024], Fc[:, tc * 1024:(tc + 1) * 1024],
                Fa[:, tc * 1024:(tc + 1) * 1024], ALU.add), inc=(tc == TC - 1))
        ev_store = G(lambda e, tt=t: e.dma_start(
            dy.ap()[tt * TT:(tt + 1) * TT, :].rearrange("(tc p) f -> p tc f", p=128),
            Fc[:, :].rearrange("p (tc f) -> p tc f", f=D)),
            waits=[ev_out], qsem="qst")
        ev_fa_free = ev_out
        if t + 1 < NT:
            AB_stash = emit_AB(ev_tnorm_next, first=False)

    except _StopEmit:
        pass

    if CNT["qst"] == 0:
        ev_dummy = G(lambda e: e.dma_start(
            dy.ap()[0:TT, :].rearrange("(tc p) f -> p tc f", p=128),
            Fx0[:, :].rearrange("p (tc f) -> p tc f", f=D)), qsem="qst")
    G(lambda e: e.wait_ge(sem["qst"], CNT["qst"]))

    block = ctx.enter_context(nc.Block())

    @block.sync
    def _(q):
        for f in prog["Q"]:
            f(q)

    @block.gpsimd
    def _(g):
        for f in prog["G"]:
            f(g)

    @block.tensor
    def _(tt):
        for f in prog["T"]:
            f(tt)

    @block.vector
    def _(v):
        for f in prog["V"]:
            f(v)

    @block.scalar
    def _(s):
        for f in prog["S"]:
            f(s)

    ctx.close()
    return nc


def make_inputs(full, b, S):
    f = full
    im = {
        "x": np.ascontiguousarray(f["L0_attn_x"][b, :S, :]),
        "gate_w": f["L0_attn_gate_w"], "state_w": f["L0_attn_state_w"],
        "out_w": f["L0_attn_out_w"], "mlp_w1": f["L0_attn_mlp_w1"],
        "mlp_w2": f["L0_attn_mlp_w2"], "sw1": f["L0_ffn_sw1"], "sw2": f["L0_ffn_sw2"],
        "router_w": f["L0_ffn_router_w"], "rw1": f["L0_ffn_rw1"], "rw2": f["L0_ffn_rw2"],
        "norm1_w": f["L0_attn_norm1_w"].reshape(1, D), "norm1_b": f["L0_attn_norm1_b"].reshape(1, D),
        "norm2_w": f["L0_attn_norm2_w"].reshape(1, D), "norm2_b": f["L0_attn_norm2_b"].reshape(1, D),
        "gate_b": f["L0_attn_gate_b"].reshape(1, 2 * D), "state_b": f["L0_attn_state_b"].reshape(1, D),
        "out_b": f["L0_attn_out_b"].reshape(1, D), "mlp_b1": f["L0_attn_mlp_b1"].reshape(1, HM),
        "mlp_b2": f["L0_attn_mlp_b2"].reshape(1, D), "router_b": f["L0_ffn_router_b"].reshape(1, NE),
        "sb1": f["L0_ffn_sb1"].reshape(1, HF), "sb2": f["L0_ffn_sb2"].reshape(1, D),
        "rb1": f["L0_ffn_rb1"], "rb2": f["L0_ffn_rb2"].reshape(1, NE * D),
        "state_b_c": f["L0_attn_state_b"].reshape(8, 128).T,
        "gate_b_c": f["L0_attn_gate_b"].reshape(16, 128).T,
        "mlp_b1_c": f["L0_attn_mlp_b1"].reshape(32, 128).T,
        "sb1_c": f["L0_ffn_sb1"].reshape(16, 128).T,
        "rb1_c": f["L0_ffn_rb1"].reshape(NE, 16, 128).transpose(2, 0, 1).reshape(128, 128),
        "n1w_c": f["L0_attn_norm1_w"].reshape(8, 128).T,
        "n1b_c": f["L0_attn_norm1_b"].reshape(8, 128).T,
        "n2w_c": f["L0_attn_norm2_w"].reshape(8, 128).T,
        "n2b_c": f["L0_attn_norm2_b"].reshape(8, 128).T,
        "ident_f32": np.eye(128, dtype=np.float32),
        "ones_f": np.ones((1, 128), np.float32),
    }
    import ml_dtypes
    im["ones_b"] = np.ones((1, 128), ml_dtypes.bfloat16)
    im = {k: np.ascontiguousarray(v) for k, v in im.items()}
    return im


def kernel(**inputs):
    S = inputs["L0_attn_x"].shape[1]
    B = inputs["L0_attn_x"].shape[0]
    nc = build(S)
    in_maps = [make_inputs(inputs, b, S) for b in range(B)]
    res = run_bass_kernel_spmd(nc, in_maps, core_ids=list(range(B)))
    out = np.stack([res.results[b]["y"] for b in range(B)], axis=0)
    return out.astype(np.float32)


# version 06
# speedup vs baseline: 1.1374x; 1.1374x over previous
import numpy as np
from contextlib import ExitStack

import concourse.bass as bass
import concourse.mybir as mybir
from concourse.bass_utils import run_bass_kernel_spmd

F32 = mybir.dt.float32
BF16 = mybir.dt.bfloat16
AF = mybir.ActivationFunctionType
ALU = mybir.AluOpType
AX = mybir.AxisListType

D = 1024
DC = 8
TT = 512
TC = 4
HM = 4096
HF = 2048
NE = 8
EPS = 1e-5
SLOT_PP = 8192
NSLOT = 2
STOP_AFTER = None


class _StopEmit(Exception):
    pass


def _pieces_manifest():
    man = []
    man.append(("gateA", "gate_w", 0, D, 0, 1024))
    man.append(("gateB", "gate_w", 0, D, 1024, 1024))
    man.append(("state", "state_w", 0, D, 0, 1024))
    man.append(("outw", "out_w", 0, D, 0, 1024))
    for q in range(4):
        man.append((f"mlp1_{q}", "mlp_w1", 0, D, q * 1024, 1024))
    for fh in range(2):
        for kh in range(2):
            man.append((f"mlp2_{kh}{fh}", "mlp_w2", kh * 2048, 2048, fh * 512, 512))
    man.append(("sw1A", "sw1", 0, D, 0, 1024))
    man.append(("sw1B", "sw1", 0, D, 1024, 1024))
    for fh in range(2):
        man.append((f"sw2_{fh}", "sw2", 0, HF, fh * 512, 512))
    for e in range(NE):
        man.append((f"rw1A_{e}", ("rw1", e), 0, D, 0, 1024))
        man.append((f"rw1B_{e}", ("rw1", e), 0, D, 1024, 1024))
        for fh in range(2):
            man.append((f"rw2_{fh}_{e}", ("rw2", e), 0, HF, fh * 512, 512))
    return man


def build(S):
    NT = S // TT
    nc = bass.Bass("TRN2", target_bir_lowering=False)

    dx = nc.dram_tensor("x", [S, D], F32, kind="ExternalInput")
    dy = nc.dram_tensor("y", [S, D], F32, kind="ExternalOutput")
    dw = {}
    for nm, shp in [
        ("gate_w", [D, 2 * D]), ("state_w", [D, D]), ("out_w", [D, D]),
        ("mlp_w1", [D, HM]), ("mlp_w2", [HM, D]),
        ("sw1", [D, HF]), ("sw2", [HF, D]), ("router_w", [D, NE]),
        ("rw1", [NE, D, HF]), ("rw2", [NE, HF, D]),
        ("norm1_w", [1, D]), ("norm1_b", [1, D]), ("norm2_w", [1, D]), ("norm2_b", [1, D]),
        ("gate_b", [1, 2 * D]), ("state_b", [1, D]), ("out_b", [1, D]),
        ("mlp_b1", [1, HM]), ("mlp_b2", [1, D]), ("router_b", [1, NE]),
        ("sb1", [1, HF]), ("sb2", [1, D]), ("rb1", [NE, HF]), ("rb2", [1, NE * D]),
        ("state_b_c", [128, 8]), ("gate_b_c", [128, 16]), ("mlp_b1_c", [128, 32]),
        ("sb1_c", [128, 16]), ("rb1_c", [128, 128]),
        ("n1w_c", [128, 8]), ("n1b_c", [128, 8]), ("n2w_c", [128, 8]), ("n2b_c", [128, 8]),
    ]:
        dw[nm] = nc.dram_tensor(nm, shp, F32, kind="ExternalInput")
    d_id = nc.dram_tensor("ident_f32", [128, 128], F32, kind="ExternalInput")
    d_ones_f = nc.dram_tensor("ones_f", [1, 128], F32, kind="ExternalInput")
    d_ones_b = nc.dram_tensor("ones_b", [1, 128], BF16, kind="ExternalInput")

    man = _pieces_manifest()
    NPIECE = len(man)
    wcache = nc.dram_tensor("wcache", [NPIECE, 128, SLOT_PP], BF16, kind="Internal")

    ctx = ExitStack()
    sb = lambda n, s, d=F32: ctx.enter_context(nc.sbuf_tensor(n, s, d))

    Fx0 = sb("Fx0", [128, 4096])
    Ftn = sb("Ftn", [128, 4096])
    Fa = sb("Fa", [128, 4096])
    Fb = sb("Fb", [128, 4096])
    Fc = sb("Fc", [128, 4096])
    Bxn = sb("Bxn", [128, 4096], BF16)
    Bxg = sb("Bxg", [128, 4096], BF16)
    Bgg = sb("Bgg", [128, 4096], BF16)
    Bys = sb("Bys", [128, 4096], BF16)
    Bh0 = sb("Bh0", [128, 8192], BF16)
    Bh1 = sb("Bh1", [128, 8192], BF16)
    WS = [ctx.enter_context(nc.sbuf_tensor(f"WS{i}", [128, SLOT_PP], BF16)) for i in range(NSLOT)]
    ident = sb("ident", [128, 128])
    ones_f = sb("ones_fs", [1, 128])
    ones_b = sb("ones_bs", [1, 128], BF16)
    rowsb = sb("rowsb", [1, 4096], BF16)
    rb2all = sb("rb2all", [8, 1024], BF16)
    w2kT = sb("w2kT", [8, 512], BF16)
    routw = sb("routw", [128, 64], BF16)
    colsb = sb("colsb", [128, 256])
    CB_STATE, CB_GATE, CB_MLP1, CB_SB1, CB_RB1 = 0, 8, 24, 56, 72
    CB_N1W, CB_N1B, CB_N2W, CB_N2B = 200, 208, 216, 224
    bbar = sb("bbar", [1, 1])
    epsc = sb("epsc", [128, 1])
    wbar = sb("wbar", [128, 8], BF16)
    wbarf = sb("wbarf", [128, 8])
    drA = sb("drA", [1, 512])
    drB = sb("drB", [1, 512])
    drep = sb("drep", [128, 512])
    darep = sb("darep", [128, 512])
    hcol = sb("hcol", [128, 16])
    stt = sb("stt", [128, 48])
    sttb = sb("sttb", [128, 48])
    st2b = sb("st2b", [128, 8])
    smcb = sb("smcb", [128, 24])
    st2 = sb("st2", [128, 8])
    sttc = sb("sttc", [128, 48])
    st2c = sb("st2c", [128, 8])
    smc = sb("smc", [128, 40])
    w2k = sb("w2k", [128, 32])
    pxp = sb("pxp", [128, 32])

    psum = ctx.enter_context(nc.psum_tensor("pall", [128, 4096], F32))
    BK = lambda i: psum[:, 512 * i:512 * (i + 1)]

    sem = {}
    for s in ["sT", "sV", "sS", "qw0", "qw1", "qc0", "qc1", "qx", "qst", "qg0", "qg1", "qb", "qgb"]:
        sem[s] = ctx.enter_context(nc.semaphore(s))
    CNT = {k: 0 for k in sem}
    prog = {k: [] for k in ["T", "V", "S", "Q", "G"]}
    need_drain = {"V": False, "S": False}

    def emit(E, fn, waits=(), inc=None, drain=False):
        ws = [w for w in waits if w is not None]
        if drain:
            prog[E].append(lambda e: e.drain())
        val = None
        if inc is not None:
            iname, iamt = inc
            CNT[iname] += iamt
            val = (iname, CNT[iname])

        def g(e, _ws=tuple(ws), _fn=fn, _inc=inc):
            for s, v in _ws:
                e.wait_ge(sem[s], v)
            inst = _fn(e)
            if _inc is not None:
                inst.then_inc(sem[_inc[0]], _inc[1])
        prog[E].append(g)
        return val

    def T(fn, waits=(), inc=False):
        return emit("T", fn, waits, ("sT", 1) if inc else None)

    def V(fn, waits=(), inc=False, drain=False):
        return emit("V", fn, waits, ("sV", 1) if inc else None, drain=drain)

    def Sc(fn, waits=(), inc=False, drain=False):
        return emit("S", fn, waits, ("sS", 1) if inc else None, drain=drain)

    def Q(fn, waits=(), qsem=None):
        return emit("Q", fn, waits, (qsem, 16) if qsem else None)

    def G(fn, waits=(), qsem=None):
        return emit("G", fn, waits, (qsem, 16) if qsem else None)

    bank_free = [[] for _ in range(8)]

    def use_bank(i):
        evs = bank_free[i]
        bank_free[i] = []
        return evs

    Q(lambda e: e.dma_start(ident[:, :], d_id.ap()), qsem="qb")
    Q(lambda e: e.dma_start(ones_f[:, :], d_ones_f.ap()), qsem="qb")
    Q(lambda e: e.dma_start(ones_b[:, :], d_ones_b.ap()), qsem="qb")
    Q(lambda e: e.dma_start(Ftn[0:1, 0:D], dw["state_b"].ap()), qsem="qb")

    def col_load(dst_c0, src_name, n_chunks):
        Q(lambda e, a=dw[src_name], c0=dst_c0, nch=n_chunks: e.dma_start(
            colsb[:, c0:c0 + nch], a.ap()), qsem="qb")

    col_load(CB_STATE, "state_b_c", 8)
    col_load(CB_GATE, "gate_b_c", 16)
    col_load(CB_MLP1, "mlp_b1_c", 32)
    col_load(CB_SB1, "sb1_c", 16)
    col_load(CB_RB1, "rb1_c", 128)
    col_load(CB_N1W, "n1w_c", 8)
    col_load(CB_N1B, "n1b_c", 8)
    col_load(CB_N2W, "n2w_c", 8)
    col_load(CB_N2B, "n2b_c", 8)
    ev_qb = ("qb", CNT["qb"])

    G(lambda e: e.dma_start(rowsb[:, 0:1024], dw["out_b"].ap()), qsem="qgb")
    G(lambda e: e.dma_start(rowsb[:, 1024:2048], dw["mlp_b2"].ap()), qsem="qgb")
    G(lambda e: e.dma_start(rowsb[:, 2048:3072], dw["sb2"].ap()), qsem="qgb")
    G(lambda e: e.dma_start(rowsb[:, 3072:3072 + NE], dw["router_b"].ap()), qsem="qgb")
    G(lambda e: e.dma_start(rb2all[:, :],
                            dw["rb2"].ap()[0:1, :].rearrange("o (e f) -> (o e) f", f=D)), qsem="qgb")
    G(lambda e: e.dma_start(
        routw[:, :].rearrange("p (c j) -> p c j", j=NE),
        dw["router_w"].ap().rearrange("(c p) j -> p c j", p=128),
        allow_slow_non_contiguous=True), qsem="qgb")
    ev_qgb = ("qgb", CNT["qgb"])

    ev_x = G(lambda e: e.dma_start(
        Fx0[:, :].rearrange("p (tc f) -> p tc f", f=D),
        dx.ap()[0:TT, :].rearrange("(tc p) f -> p tc f", p=128)), qsem="qx")

    V(lambda e: e.memset(epsc[:, :], EPS))
    V(lambda e: e.reduce_sum(bbar[:, :], Ftn[0:1, 0:D], axis=AX.X), waits=[ev_qb])
    ev_bbar_pre = V(lambda e: e.tensor_scalar(bbar[:, :], bbar[:, :], 1.0 / D, None, ALU.mult),
                    inc=True, drain=True)

    piece_src = {}
    for i, (key, src, k0, K, n0, N) in enumerate(man):
        piece_src[key] = (i, src, k0, K, n0, N)

    slot_free = [[] for _ in range(NSLOT)]
    slot_rr = [0]
    piece_ev = {}
    cache_store_ev = {}

    def load_piece(t, key):
        idx, src, k0, K, n0, N = piece_src[key]
        nch = K // 128
        npp = nch * N
        sl = slot_rr[0]
        slot_rr[0] = (sl + 1) % NSLOT
        dst = WS[sl][:, :npp].rearrange("p (c n) -> p c n", n=N)
        if isinstance(src, tuple):
            srct = dw[src[0]].ap()[src[1]]
        else:
            srct = dw[src].ap()
        srcap = srct[k0:k0 + K, n0:n0 + N].rearrange("(c p) n -> p c n", p=128)
        frees = slot_free[sl]
        slot_free[sl] = []
        if t == 0:
            ev = G(lambda e, d=dst, s=srcap: e.dma_start(d, s),
                   waits=frees, qsem=f"qg{sl}")
            sev = Q(lambda e, i2=idx, npp2=npp, sl2=sl: e.dma_start(
                wcache.ap()[i2][:, :npp2], WS[sl2][:, :npp2]),
                waits=[ev], qsem=f"qc{sl}")
            cache_store_ev[idx] = sev
            slot_free[sl].append(sev)
        else:
            ws = list(frees)
            if t == 1:
                ws.append(cache_store_ev[idx])
            ev = Q(lambda e, d=dst, s=wcache.ap()[idx][:, :npp]: e.dma_start(d, s),
                   waits=ws, qsem=f"qw{sl}")
        piece_ev[key] = (sl, ev)
        return sl, ev

    def consume_piece(key, tev):
        sl, _ = piece_ev[key]
        slot_free[sl].append(tev)

    ev_store = None
    wbar_ready = None

    piece_index = {m[0]: i for i, m in enumerate(man)}

    def ck(name):
        if STOP_AFTER is not None and name == STOP_AFTER:
            raise _StopEmit()

    def emit_A_stats(wait_ev):
        for tc in range(TC):
            for h in range(2):
                V(lambda e, tc=tc, h=h: e.bn_stats(
                    sttb[:, tc * 12 + 6 * h: tc * 12 + 6 * h + 6],
                    Fx0[:, tc * 1024 + 512 * h: tc * 1024 + 512 * (h + 1)]),
                  waits=[wait_ev] if (tc == 0 and h == 0) else (),
                  drain=(tc == 0 and h == 0))
        for tc in range(TC):
            V(lambda e, tc=tc: e.bn_aggr(st2b[:, tc * 2:tc * 2 + 2], sttb[:, tc * 12:tc * 12 + 12]),
              drain=(tc == 0))
        ev_st = V(lambda e: e.tensor_copy(smcb[:, 0:4], st2b[:, 1:8:2]), inc=True, drain=True)
        Sc(lambda e: e.activation(smcb[:, 4:8], smcb[:, 0:4], AF.Ln, bias=epsc[:, 0:1]), waits=[ev_st])
        ev_r0 = Sc(lambda e: e.activation(smcb[:, 8:12], smcb[:, 4:8], AF.Exp, scale=-0.5),
                   inc=True, drain=True)
        V(lambda e: e.tensor_tensor(smcb[:, 12:16], smcb[:, 8:12], smcb[:, 8:12], ALU.mult),
          waits=[ev_r0], drain=True)
        ev_v1 = V(lambda e: e.tensor_tensor(smcb[:, 12:16], smcb[:, 12:16], smcb[:, 0:4], ALU.mult),
                  inc=True, drain=True)
        Sc(lambda e: e.activation(smcb[:, 16:20], smcb[:, 12:16], AF.Ln, bias=epsc[:, 0:1]),
           waits=[ev_v1], drain=True)
        ev_r1 = Sc(lambda e: e.activation(smcb[:, 16:20], smcb[:, 16:20], AF.Exp, scale=-0.5),
                   inc=True, drain=True)
        return V(lambda e: e.tensor_tensor(smcb[:, 20:24], smcb[:, 8:12], smcb[:, 16:20], ALU.mult),
                 waits=[ev_r1], inc=True, drain=True)

    def emit_Anorm(waits):
        ev = None
        for tc in range(TC):
            ev = V(lambda e, tc=tc: e.tensor_scalar(
                Ftn[:, tc * 1024:(tc + 1) * 1024], Fx0[:, tc * 1024:(tc + 1) * 1024],
                st2b[:, 2 * tc:2 * tc + 1], smcb[:, 20 + tc:21 + tc],
                ALU.subtract, ALU.mult), waits=waits if tc == 0 else (), inc=(tc == TC - 1),
                drain=(tc == 0))
        return ev

    def emit_AB(ev_tnorm, first):
        ev_trs = {}
        for fc in range(DC):
            wt = use_bank(fc) + ([ev_tnorm] if fc == 0 else [])
            if first and fc == 0:
                wt += [ev_qb, ev_qgb]
            for tcj in range(TC):
                ev_trs[fc] = T(lambda e, fc=fc, tcj=tcj: e.transpose(
                    BK(fc)[:, tcj * 128:(tcj + 1) * 128],
                    Ftn[:, tcj * 1024 + fc * 128: tcj * 1024 + (fc + 1) * 128],
                    ident[:, :]), waits=wt if tcj == 0 else (), inc=(tcj == TC - 1))
        ev_xnf = {}
        ev_xnb = None
        for fc in range(DC):
            ev_xnf[fc] = Sc(lambda e, fc=fc: e.activation(
                Ftn[:, fc * 512:(fc + 1) * 512], BK(fc), AF.Identity,
                scale=colsb[:, CB_N1W + fc:CB_N1W + fc + 1],
                bias=colsb[:, CB_N1B + fc:CB_N1B + fc + 1]),
                waits=[ev_trs[DC - 1]] if fc == 0 else (), inc=True)
            bank_free[fc].append(ev_xnf[fc])
            ev_xnb = V(lambda e, fc=fc: e.tensor_copy(
                Bxn[:, fc * 512:(fc + 1) * 512], Ftn[:, fc * 512:(fc + 1) * 512]),
                waits=[ev_xnf[fc]], inc=True)
        return ev_xnb, ev_xnf

    emit_A_stats(ev_x)
    AB_stash = None
    ev_fa_free = None

    try:
      for t in range(NT):
        par = t % 2
        next_pi = [0]

        def ensure(key, ahead=1, _t=t, _next=None):
            target = min(piece_index[key] + ahead, NPIECE - 1)
            while next_pi[0] <= target:
                load_piece(t, man[next_pi[0]][0])
                next_pi[0] += 1

        if t == 0:
            ev_tnorm0 = emit_Anorm([])
            ev_xnb, ev_xnf = emit_AB(ev_tnorm0, first=True)
        else:
            ev_xnb, ev_xnf = AB_stash

        ck("B")
        ev_xg = {}
        ev_gg = {}
        for half, key in enumerate(["gateA", "gateB"]):
            ensure(key)
            slg, evg = piece_ev[key]
            ev_last_mm = None
            for fo in range(8):
                bk = fo % 4
                wt = use_bank(bk) + ([evg, ev_xnb] if fo in (0, 1, 2, 3) else [])
                for kc in range(DC):
                    ev_last_mm = T(lambda e, bk=bk, slg=slg, kc=kc, fo=fo: e.matmul(
                        BK(bk), WS[slg][:, kc * 1024 + fo * 128: kc * 1024 + (fo + 1) * 128],
                        Bxn[:, kc * 512:(kc + 1) * 512],
                        start=(kc == 0), stop=(kc == DC - 1)),
                        waits=wt if kc == 0 else (), inc=(kc == DC - 1))
                if half == 0:
                    evs = Sc(lambda e, bk=bk, fo=fo: e.activation(
                        BK(bk), BK(bk), AF.Silu,
                        bias=colsb[:, CB_GATE + fo:CB_GATE + fo + 1]),
                        waits=[ev_last_mm], inc=True)
                    evv = V(lambda e, bk=bk, fo=fo: e.tensor_tensor(
                        Bxg[:, fo * 512:(fo + 1) * 512], BK(bk),
                        Ftn[:, fo * 512:(fo + 1) * 512], ALU.mult),
                        waits=[evs], inc=True)
                    ev_xg[fo] = evv
                    bank_free[bk].append(evv)
                else:
                    evs = Sc(lambda e, bk=bk, fo=fo: e.activation(
                        Bgg[:, fo * 512:(fo + 1) * 512], BK(bk), AF.Silu,
                        bias=colsb[:, CB_GATE + 8 + fo:CB_GATE + 8 + fo + 1]),
                        waits=[ev_last_mm], inc=True)
                    ev_gg[fo] = evs
                    bank_free[bk].append(evs)
            consume_piece(key, ev_last_mm)

        ck("C")
        ensure("state")
        sls, evsl = piece_ev["state"]
        ev_u = {}
        ev_bm = {}
        ev_last_state_mm = None
        for fo in range(8):
            for which in range(2):
                bk = (2 * fo + which) % 4
                rhs_buf = Bxg if which == 0 else Bgg
                rdy = ev_xg[7] if which == 0 else ev_gg[7]
                wt = use_bank(bk) + ([evsl, rdy] if fo == 0 else [])
                for kc in range(DC):
                    ev_last_state_mm = T(lambda e, bk=bk, sls=sls, kc=kc, fo=fo, rb=rhs_buf: e.matmul(
                        BK(bk), WS[sls][:, kc * 1024 + fo * 128: kc * 1024 + (fo + 1) * 128],
                        rb[:, kc * 512:(kc + 1) * 512],
                        start=(kc == 0), stop=(kc == DC - 1)),
                        waits=wt if kc == 0 else (), inc=(kc == DC - 1))
                dstb = Fa if which == 0 else Fb
                ew = [ev_last_state_mm]
                if which == 0 and fo == 0:
                    ew.append(ev_fa_free)
                evs = Sc(lambda e, bk=bk, fo=fo, dstb=dstb: e.activation(
                    dstb[:, fo * 512:(fo + 1) * 512], BK(bk), AF.Identity,
                    bias=colsb[:, CB_STATE + fo:CB_STATE + fo + 1]),
                    waits=ew, inc=True)
                bank_free[bk].append(evs)
                if which == 0:
                    ev_u[fo] = evs
                else:
                    ev_bm[fo] = evs
        consume_piece("state", ev_last_state_mm)

        ck("E")
        if t == 0:
            slst = sls
            for kc in range(DC):
                V(lambda e, kc=kc, s=slst: e.reduce_sum(
                    wbarf[:, kc:kc + 1], WS[s][:, kc * 1024:(kc + 1) * 1024], axis=AX.X),
                  waits=[evsl] if kc == 0 else ())
            V(lambda e: e.tensor_scalar(wbarf[:, :], wbarf[:, :], 1.0 / D, None, ALU.mult), drain=True)
            wbar_ready = V(lambda e: e.tensor_copy(wbar[:, :], wbarf[:, :]), inc=True, drain=True)
            slot_free[sls].append(wbar_ready)

        wt = use_bank(6) + [ev_xnb, wbar_ready]
        ev_dmm = None
        for kc in range(DC):
            ev_dmm = T(lambda e, kc=kc: e.matmul(
                BK(6)[0:1, :], wbar[:, kc:kc + 1], Bxn[:, kc * 512:(kc + 1) * 512],
                start=(kc == 0), stop=(kc == DC - 1)),
                waits=wt if kc == 0 else (), inc=(kc == DC - 1))
        Sc(lambda e: e.activation(drA[0:1, :], BK(6)[0:1, :], AF.Exp, bias=bbar[0:1, 0:1]),
           waits=[ev_dmm, ev_bbar_pre])
        ev_dsp = Sc(lambda e: e.activation(drB[0:1, :], drA[0:1, :], AF.Ln, bias=1.0),
                    inc=True, drain=True)
        ev_dA = Sc(lambda e: e.activation(drA[0:1, :], drB[0:1, :], AF.Exp, scale=-1.0),
                   inc=True, drain=True)
        bank_free[6].append(ev_dsp)
        wt6 = use_bank(6)
        ev_rp1 = T(lambda e: e.matmul(BK(6), ones_f[0:1, :], drB[0:1, :], start=True, stop=True),
                   waits=wt6 + [ev_dsp], inc=True)
        wt7 = use_bank(7)
        ev_rp2 = T(lambda e: e.matmul(BK(7), ones_f[0:1, :], drA[0:1, :], start=True, stop=True),
                   waits=wt7 + [ev_dA], inc=True)
        ev_drep = V(lambda e: e.tensor_copy(drep[:, :], BK(6)), waits=[ev_rp1], inc=True)
        ev_darep = V(lambda e: e.tensor_copy(darep[:, :], BK(7)), waits=[ev_rp2], inc=True)
        bank_free[6].append(ev_drep)
        bank_free[7].append(ev_darep)

        ck("D")
        for fo in range(8):
            wdep = [ev_u[fo], ev_bm[fo]]
            if ev_store is not None and fo == 0:
                wdep.append(ev_store)
            V(lambda e, fo=fo: e.tensor_tensor(
                Fc[:, fo * 512:(fo + 1) * 512], Fa[:, fo * 512:(fo + 1) * 512],
                Fb[:, fo * 512:(fo + 1) * 512], ALU.mult), waits=wdep, drain=(fo == 0))
        ev_dbu = {}
        for fo in range(8):
            ev_dbu[fo] = V(lambda e, fo=fo: e.tensor_tensor(
                Fc[:, fo * 512:(fo + 1) * 512], Fc[:, fo * 512:(fo + 1) * 512],
                drep[:, :], ALU.mult), waits=[ev_drep] if fo == 0 else (),
                inc=True, drain=(fo == 0))

        ev_scan = {}
        for fo in range(8):
            if t == 0:
                init = 0.0
            else:
                init = hcol[:, (1 - par) * 8 + fo:(1 - par) * 8 + fo + 1]
            ev_scan[fo] = V(lambda e, fo=fo, init=init: e.tensor_tensor_scan(
                Bys[:, fo * 512:(fo + 1) * 512], darep[:, :],
                Fc[:, fo * 512:(fo + 1) * 512], init, ALU.mult, ALU.add),
                waits=[ev_darep] if fo == 0 else (), inc=True, drain=(fo == 0))
        for fo in range(8):
            V(lambda e, fo=fo, par=par: e.tensor_copy(
                hcol[:, par * 8 + fo:par * 8 + fo + 1],
                Bys[:, fo * 512 + 511:fo * 512 + 512]), drain=(fo == 0))

        ck("F")
        ensure("outw")
        slo, evo = piece_ev["outw"]
        ev_mid = {}
        ev_last_out_mm = None
        ev_xin = None
        for tc in range(TC):
            ev_xin = V(lambda e, tc=tc: e.tensor_scalar(
                Ftn[:, tc * 1024:(tc + 1) * 1024], Fx0[:, tc * 1024:(tc + 1) * 1024],
                st2b[:, 2 * tc:2 * tc + 1], smcb[:, 8 + tc:9 + tc],
                ALU.subtract, ALU.mult),
                waits=[ev_xnf[DC - 1]] if tc == 0 else (), inc=(tc == TC - 1),
                drain=(tc == 0))
        for tc in range(TC):
            for nh in range(2):
                bk = 2 + ((2 * tc + nh) % 4)
                wt = use_bank(bk) + ([evo, ev_scan[7]] if tc == 0 else [])
                for kc in range(DC):
                    T(lambda e, bk=bk, slo=slo, kc=kc, tc=tc, nh=nh: e.matmul(
                        BK(bk), Bys[:, kc * 512 + tc * 128: kc * 512 + (tc + 1) * 128],
                        WS[slo][:, kc * 1024 + nh * 512: kc * 1024 + (nh + 1) * 512],
                        start=(kc == 0), stop=False),
                        waits=wt if kc == 0 else ())
                ev_last_out_mm = T(lambda e, bk=bk, nh=nh: e.matmul(
                    BK(bk), ones_b[0:1, :], rowsb[0:1, nh * 512:(nh + 1) * 512],
                    start=False, stop=True), inc=True)
                evv = V(lambda e, bk=bk, tc=tc, nh=nh: e.tensor_tensor(
                    Fc[:, tc * 1024 + nh * 512: tc * 1024 + (nh + 1) * 512], BK(bk),
                    Ftn[:, tc * 1024 + nh * 512: tc * 1024 + (nh + 1) * 512], ALU.add),
                    waits=[ev_last_out_mm, ev_xin] if (tc == 0 and nh == 0) else [ev_last_out_mm],
                    inc=True, drain=(tc == 0 and nh == 0))
                ev_mid[(tc, nh)] = evv
                bank_free[bk].append(evv)
                V(lambda e, tc=tc, nh=nh: e.bn_stats(
                    stt[:, tc * 12 + 6 * nh: tc * 12 + 6 * nh + 6],
                    Fc[:, tc * 1024 + 512 * nh: tc * 1024 + 512 * (nh + 1)]), drain=True)
                if nh == 1:
                    V(lambda e, tc=tc: e.bn_aggr(st2[:, tc * 2:tc * 2 + 2],
                                                 stt[:, tc * 12:tc * 12 + 12]), drain=True)
        consume_piece("outw", ev_last_out_mm)
        ev_mid_all = ev_mid[(TC - 1, 1)]

        ck("G")
        ev_st = V(lambda e: e.engine_nop(), inc=True)
        Sc(lambda e: e.activation(smc[:, 4:8], st2[:, 1:8:2], AF.Ln, bias=epsc[:, 0:1]),
           waits=[ev_st])
        ev_r2 = Sc(lambda e: e.activation(smc[:, 8:12], smc[:, 4:8], AF.Exp, scale=-0.5),
                   inc=True, drain=True)
        ev_tnorm2 = None
        for tc in range(TC):
            ev_tnorm2 = V(lambda e, tc=tc: e.tensor_scalar(
                Ftn[:, tc * 1024:(tc + 1) * 1024], Fc[:, tc * 1024:(tc + 1) * 1024],
                st2[:, 2 * tc:2 * tc + 1], smc[:, 8 + tc:9 + tc],
                ALU.subtract, ALU.mult), waits=[ev_r2] if tc == 0 else (), inc=(tc == TC - 1),
                drain=(tc == 0))
        ev_x2 = {}
        for fc in range(DC):
            bk = fc
            wt = use_bank(bk) + ([ev_tnorm2] if fc in (0, 1) else [])
            for tcj in range(TC):
                ev_tr = T(lambda e, bk=bk, tcj=tcj, fc=fc: e.transpose(
                    BK(bk)[:, tcj * 128:(tcj + 1) * 128],
                    Ftn[:, tcj * 1024 + fc * 128: tcj * 1024 + (fc + 1) * 128],
                    ident[:, :]), waits=wt if tcj == 0 else (), inc=(tcj == TC - 1))
            ev_x2[fc] = Sc(lambda e, bk=bk, fc=fc: e.activation(
                Bxn[:, fc * 512:(fc + 1) * 512], BK(bk), AF.Identity,
                scale=colsb[:, CB_N2W + fc:CB_N2W + fc + 1],
                bias=colsb[:, CB_N2B + fc:CB_N2B + fc + 1]), waits=[ev_tr], inc=True)
            bank_free[bk].append(ev_x2[fc])
        ev_x2_all = ev_x2[DC - 1]

        ck("H")
        ev_h1 = {}
        for q in range(4):
            key = f"mlp1_{q}"
            ensure(key)
            slq, evq = piece_ev[key]
            ev_last = None
            for fl in range(8):
                hc = q * 8 + fl
                bk = hc % 4
                wt = use_bank(bk) + ([evq, ev_x2_all] if fl in (0, 1, 2, 3) else [])
                for kc in range(DC):
                    ev_last = T(lambda e, bk=bk, slq=slq, kc=kc, fl=fl: e.matmul(
                        BK(bk), WS[slq][:, kc * 1024 + fl * 128: kc * 1024 + (fl + 1) * 128],
                        Bxn[:, kc * 512:(kc + 1) * 512],
                        start=(kc == 0), stop=(kc == DC - 1)),
                        waits=wt if kc == 0 else (), inc=(kc == DC - 1))
                dsth = Bh0 if hc < 16 else Bh1
                evs = Sc(lambda e, bk=bk, hc=hc, dsth=dsth: e.activation(
                    dsth[:, (hc % 16) * 512:((hc % 16) + 1) * 512], BK(bk), AF.Gelu,
                    bias=colsb[:, CB_MLP1 + hc:CB_MLP1 + hc + 1]),
                    waits=[ev_last], inc=True)
                ev_h1[hc] = evs
                bank_free[bk].append(evs)
            consume_piece(key, ev_last)

        ck("I")
        ev_bout = {}
        for fh in range(2):
            for kh in range(2):
                key = f"mlp2_{kh}{fh}"
                ensure(key)
                slm, evm = piece_ev[key]
                hbuf = Bh0 if kh == 0 else Bh1
                ev_pl = None
                for tc in range(TC):
                    bk = 2 + tc
                    if kh == 0:
                        wt = use_bank(bk) + [evm, ev_h1[15]]
                    else:
                        wt = [evm, ev_h1[31]]
                    for kl in range(16):
                        ev_pl = T(lambda e, bk=bk, slm=slm, kl=kl, tc=tc, hbuf=hbuf, kh=kh: e.matmul(
                            BK(bk), hbuf[:, kl * 512 + tc * 128: kl * 512 + (tc + 1) * 128],
                            WS[slm][:, kl * 512:(kl + 1) * 512],
                            start=(kh == 0 and kl == 0), stop=False),
                            waits=wt if kl == 0 else (),
                            inc=(tc == TC - 1 and kl == 15))
                consume_piece(key, ev_pl)
            evb = None
            for tc in range(TC):
                bk = 2 + tc
                evb = T(lambda e, bk=bk, fh=fh: e.matmul(
                    BK(bk), ones_b[0:1, :], rowsb[0:1, 1024 + fh * 512:1024 + (fh + 1) * 512],
                    start=False, stop=True), inc=True)
                evv = V(lambda e, bk=bk, tc=tc, fh=fh: e.tensor_tensor(
                    Fc[:, tc * 1024 + fh * 512: tc * 1024 + (fh + 1) * 512], BK(bk),
                    Fc[:, tc * 1024 + fh * 512: tc * 1024 + (fh + 1) * 512], ALU.add),
                    waits=[evb], inc=True, drain=(tc == 0 and fh == 0))
                ev_bout[(tc, fh)] = evv
                bank_free[bk].append(evv)
                V(lambda e, tc=tc, fh=fh: e.bn_stats(
                    sttc[:, tc * 12 + 6 * fh: tc * 12 + 6 * fh + 6],
                    Fc[:, tc * 1024 + 512 * fh: tc * 1024 + 512 * (fh + 1)]), drain=True)
                if fh == 1:
                    V(lambda e, tc=tc: e.bn_aggr(st2c[:, tc * 2:tc * 2 + 2],
                                                 sttc[:, tc * 12:tc * 12 + 12]), drain=True)

        ck("J")
        ev_st = V(lambda e: e.engine_nop(), inc=True)
        Sc(lambda e: e.activation(smc[:, 4:8], st2c[:, 1:8:2], AF.Ln, bias=epsc[:, 0:1]),
           waits=[ev_st])
        ev_r3 = Sc(lambda e: e.activation(smc[:, 8:12], smc[:, 4:8], AF.Exp, scale=-0.5),
                   inc=True, drain=True)
        ev_x1 = None
        for tc in range(TC):
            V(lambda e, tc=tc: e.tensor_scalar(
                Ftn[:, tc * 1024:(tc + 1) * 1024], Fc[:, tc * 1024:(tc + 1) * 1024],
                st2c[:, 2 * tc:2 * tc + 1], smc[:, 8 + tc:9 + tc],
                ALU.subtract, ALU.mult), waits=[ev_r3] if tc == 0 else (), drain=(tc == 0))
        for tc in range(TC):
            ev_x1 = V(lambda e, tc=tc: e.tensor_tensor(
                Fa[:, tc * 1024:(tc + 1) * 1024], Ftn[:, tc * 1024:(tc + 1) * 1024],
                Fx0[:, tc * 1024:(tc + 1) * 1024], ALU.add), inc=(tc == TC - 1),
                drain=(tc == 0))
        for tc in range(TC):
            for h in range(2):
                V(lambda e, tc=tc, h=h: e.bn_stats(
                    stt[:, tc * 12 + 6 * h: tc * 12 + 6 * h + 6],
                    Fa[:, tc * 1024 + 512 * h: tc * 1024 + 512 * (h + 1)]),
                  drain=(tc == 0 and h == 0))
        for tc in range(TC):
            V(lambda e, tc=tc: e.bn_aggr(st2[:, tc * 2:tc * 2 + 2], stt[:, tc * 12:tc * 12 + 12]),
              drain=(tc == 0))
        if t + 1 < NT:
            ev_x = G(lambda e, tt=t + 1: e.dma_start(
                Fx0[:, :].rearrange("p (tc f) -> p tc f", f=D),
                dx.ap()[tt * TT:(tt + 1) * TT, :].rearrange("(tc p) f -> p tc f", p=128)),
                waits=[ev_x1], qsem="qx")
            emit_A_stats(ev_x)

        ck("K")
        ev_st = V(lambda e: e.engine_nop(), inc=True)
        Sc(lambda e: e.activation(smc[:, 4:8], st2[:, 1:8:2], AF.Ln, bias=epsc[:, 0:1]),
           waits=[ev_st])
        ev_r4 = Sc(lambda e: e.activation(smc[:, 8:12], smc[:, 4:8], AF.Exp, scale=-0.5),
                   inc=True, drain=True)
        ev_tnorm4 = None
        for tc in range(TC):
            ev_tnorm4 = V(lambda e, tc=tc: e.tensor_scalar(
                Ftn[:, tc * 1024:(tc + 1) * 1024], Fa[:, tc * 1024:(tc + 1) * 1024],
                st2[:, 2 * tc:2 * tc + 1], smc[:, 8 + tc:9 + tc],
                ALU.subtract, ALU.mult), waits=[ev_r4] if tc == 0 else (), inc=(tc == TC - 1),
                drain=(tc == 0))
        ev_fin = {}
        for fc in range(DC):
            bk = fc
            wt = use_bank(bk) + ([ev_tnorm4] if fc in (0, 1) else [])
            for tcj in range(TC):
                ev_tr = T(lambda e, bk=bk, tcj=tcj, fc=fc: e.transpose(
                    BK(bk)[:, tcj * 128:(tcj + 1) * 128],
                    Ftn[:, tcj * 1024 + fc * 128: tcj * 1024 + (fc + 1) * 128],
                    ident[:, :]), waits=wt if tcj == 0 else (), inc=(tcj == TC - 1))
            ev_fin[fc] = Sc(lambda e, bk=bk, fc=fc: e.activation(
                Bgg[:, fc * 512:(fc + 1) * 512], BK(bk), AF.Identity), waits=[ev_tr], inc=True)
            bank_free[bk].append(ev_fin[fc])
        ev_fin_all = ev_fin[DC - 1]
        ev_trL = ev_tr
        ev_tnorm_next = emit_Anorm([ev_trL]) if t + 1 < NT else None

        ck("L")
        ev_rt = {}
        ev_w2k = None
        for tc in range(TC):
            bk = 6 + (tc % 2)
            wt = use_bank(bk)
            if tc == 0:
                wt += [ev_fin_all, ev_qgb]
            for kc in range(DC):
                T(lambda e, tc=tc, kc=kc, bk=bk: e.matmul(
                    BK(bk)[:, tc * 8:tc * 8 + 8],
                    Bgg[:, kc * 512 + tc * 128: kc * 512 + (tc + 1) * 128],
                    routw[:, kc * 8:(kc + 1) * 8],
                    start=(kc == 0), stop=False),
                    waits=wt if kc == 0 else ())
            ev_rt[tc] = T(lambda e, tc=tc, bk=bk: e.matmul(
                BK(bk)[:, tc * 8:tc * 8 + 8], ones_b[0:1, :], rowsb[0:1, 3072:3072 + 8],
                start=False, stop=True), inc=True)
            c0 = tc * 8
            V(lambda e, tc=tc, c0=c0, bk=bk: e.reduce_max(smc[:, 24:25], BK(bk)[:, c0:c0 + 8], axis=AX.X),
              waits=[ev_rt[tc]], drain=True)
            ev_ngm = V(lambda e: e.tensor_scalar(smc[:, 25:26], smc[:, 24:25], -1.0, None, ALU.mult),
                       inc=True, drain=True)
            evsx = Sc(lambda e, c0=c0, bk=bk: e.activation(pxp[:, c0:c0 + 8], BK(bk)[:, c0:c0 + 8],
                                                           AF.Exp, bias=smc[:, 25:26]),
                      waits=[ev_ngm], inc=True)
            bank_free[bk].append(evsx)
            V(lambda e, c0=c0: e.reduce_sum(smc[:, 26:27], pxp[:, c0:c0 + 8], axis=AX.X),
              waits=[evsx], drain=True)
            V(lambda e: e.reciprocal(smc[:, 27:28], smc[:, 26:27]), drain=True)
            V(lambda e, c0=c0: e.tensor_scalar(pxp[:, c0:c0 + 8], pxp[:, c0:c0 + 8],
                                               smc[:, 27:28], None, ALU.mult), drain=True)
            V(lambda e, c0=c0: e.reduce_max(smc[:, 24:25], pxp[:, c0:c0 + 8], axis=AX.X), drain=True)
            V(lambda e, c0=c0: e.tensor_scalar(w2k[:, c0:c0 + 8], pxp[:, c0:c0 + 8],
                                               smc[:, 24:25], None, ALU.is_ge), drain=True)
            V(lambda e, c0=c0: e.tensor_scalar(w2k[:, c0:c0 + 8], w2k[:, c0:c0 + 8],
                                               -1e30, None, ALU.mult), drain=True)
            V(lambda e, c0=c0: e.tensor_tensor(w2k[:, c0:c0 + 8], w2k[:, c0:c0 + 8],
                                               pxp[:, c0:c0 + 8], ALU.add), drain=True)
            V(lambda e, c0=c0: e.reduce_max(smc[:, 24:25], w2k[:, c0:c0 + 8], axis=AX.X), drain=True)
            V(lambda e, c0=c0: e.tensor_scalar(w2k[:, c0:c0 + 8], pxp[:, c0:c0 + 8],
                                               smc[:, 24:25], None, ALU.is_ge), drain=True)
            ev_w2k = V(lambda e, c0=c0: e.tensor_tensor(w2k[:, c0:c0 + 8], w2k[:, c0:c0 + 8],
                                                        pxp[:, c0:c0 + 8], ALU.mult),
                       inc=(tc == TC - 1), drain=True)

        wt6 = use_bank(6)
        ev_trw = None
        for tc in range(TC):
            ev_trw = T(lambda e, tc=tc: e.transpose(
                BK(6)[0:8, tc * 128:(tc + 1) * 128], w2k[:, tc * 8:tc * 8 + 8],
                ident[:, :]), waits=(wt6 + [ev_w2k]) if tc == 0 else (), inc=(tc == TC - 1))
        ev_w2kT = Sc(lambda e: e.activation(w2kT[0:8, :], BK(6)[0:8, :], AF.Identity),
                     waits=[ev_trw], inc=True)
        bank_free[6].append(ev_w2kT)

        ck("M")
        ev_hsh = {}
        for half, key in enumerate(["sw1A", "sw1B"]):
            ensure(key)
            slg, evg = piece_ev[key]
            ev_last = None
            for fl in range(8):
                hc = half * 8 + fl
                bk = hc % 4
                wt = use_bank(bk) + ([evg, ev_fin_all] if fl in (0, 1, 2, 3) else [])
                for kc in range(DC):
                    ev_last = T(lambda e, bk=bk, slg=slg, kc=kc, fl=fl: e.matmul(
                        BK(bk), WS[slg][:, kc * 1024 + fl * 128: kc * 1024 + (fl + 1) * 128],
                        Bgg[:, kc * 512:(kc + 1) * 512],
                        start=(kc == 0), stop=(kc == DC - 1)),
                        waits=wt if kc == 0 else (), inc=(kc == DC - 1))
                evs = Sc(lambda e, bk=bk, hc=hc: e.activation(
                    Bh0[:, hc * 512:(hc + 1) * 512], BK(bk), AF.Gelu,
                    bias=colsb[:, CB_SB1 + hc:CB_SB1 + hc + 1]),
                    waits=[ev_last], inc=True)
                ev_hsh[hc] = evs
                bank_free[bk].append(evs)
            consume_piece(key, ev_last)
        ck("N1")
        ev_mix = {}
        for fh in range(2):
            key = f"sw2_{fh}"
            ensure(key)
            slm, evm = piece_ev[key]
            ev_pl = None
            for tc in range(TC):
                bk = 2 + tc
                wt = use_bank(bk) + ([evm, ev_hsh[15]] if tc == 0 else [])
                for kl in range(16):
                    ev_pl = T(lambda e, bk=bk, slm=slm, kl=kl, tc=tc: e.matmul(
                        BK(bk), Bh0[:, kl * 512 + tc * 128: kl * 512 + (tc + 1) * 128],
                        WS[slm][:, kl * 512:(kl + 1) * 512],
                        start=(kl == 0), stop=False),
                        waits=wt if kl == 0 else (),
                        inc=(tc == TC - 1 and kl == 15))
                T(lambda e, bk=bk, tc=tc, fh=fh: e.matmul(
                    BK(bk), w2kT[0:8, tc * 128:(tc + 1) * 128],
                    rb2all[0:8, fh * 512:(fh + 1) * 512],
                    start=False, stop=False),
                    waits=[ev_w2kT] if (tc == 0 and fh == 0) else ())
                evb = T(lambda e, bk=bk, fh=fh: e.matmul(
                    BK(bk), ones_b[0:1, :], rowsb[0:1, 2048 + fh * 512:2048 + (fh + 1) * 512],
                    start=False, stop=True), inc=True)
                evv = V(lambda e, bk=bk, tc=tc, fh=fh: e.tensor_copy(
                    Fb[:, tc * 1024 + fh * 512: tc * 1024 + (fh + 1) * 512], BK(bk)),
                    waits=[evb], inc=True, drain=(fh == 0 and tc == 0))
                ev_mix[(tc, fh)] = evv
                bank_free[bk].append(evv)
            consume_piece(key, ev_pl)

        ck("N")
        for e_i in range(NE):
            hbuf = Bh1 if (e_i % 2 == 0) else Bh0
            ev_he = {}
            for half, key in enumerate([f"rw1A_{e_i}", f"rw1B_{e_i}"]):
                ensure(key)
                slg, evg = piece_ev[key]
                ev_last = None
                for fl in range(8):
                    hc = half * 8 + fl
                    bk = (0, 1, 6, 7)[hc % 4]
                    wt = use_bank(bk) + ([evg] if fl in (0, 1, 2, 3) else [])
                    for kc in range(DC):
                        ev_last = T(lambda e, bk=bk, slg=slg, kc=kc, fl=fl: e.matmul(
                            BK(bk), WS[slg][:, kc * 1024 + fl * 128: kc * 1024 + (fl + 1) * 128],
                            Bgg[:, kc * 512:(kc + 1) * 512],
                            start=(kc == 0), stop=(kc == DC - 1)),
                            waits=wt if kc == 0 else (), inc=(kc == DC - 1))
                    evs = Sc(lambda e, bk=bk, hc=hc, hbuf=hbuf, e_i=e_i: e.activation(
                        hbuf[:, hc * 512:(hc + 1) * 512], BK(bk), AF.Gelu,
                        bias=colsb[:, CB_RB1 + e_i * 16 + hc:CB_RB1 + e_i * 16 + hc + 1]),
                        waits=[ev_last], inc=True)
                    ev_he[hc] = evs
                    bank_free[bk].append(evs)
                consume_piece(key, ev_last)
            for fh in range(2):
                key = f"rw2_{fh}_{e_i}"
                ensure(key)
                slm, evm = piece_ev[key]
                ev_pl = None
                for tc in range(TC):
                    bk = 2 + tc
                    wt = use_bank(bk) + ([evm, ev_he[15]] if tc == 0 else [])
                    evb = None
                    for kl in range(16):
                        evb = ev_pl = T(lambda e, bk=bk, slm=slm, kl=kl, tc=tc, hbuf=hbuf: e.matmul(
                            BK(bk), hbuf[:, kl * 512 + tc * 128: kl * 512 + (tc + 1) * 128],
                            WS[slm][:, kl * 512:(kl + 1) * 512],
                            start=(kl == 0), stop=(kl == 15)),
                            waits=wt if kl == 0 else (),
                            inc=(kl == 15))
                    mixw = [evb, ev_w2k] + ([ev_mix[(tc, fh)]] if e_i == 0 else [])
                    evv = V(lambda e, bk=bk, tc=tc, fh=fh, e_i=e_i: e.scalar_tensor_tensor(
                        Fb[:, tc * 1024 + fh * 512: tc * 1024 + (fh + 1) * 512], BK(bk),
                        w2k[:, tc * 8 + e_i:tc * 8 + e_i + 1],
                        Fb[:, tc * 1024 + fh * 512: tc * 1024 + (fh + 1) * 512],
                        ALU.mult, ALU.add),
                        waits=mixw, inc=True, drain=(tc == 0 and fh == 0))
                    ev_mix[(tc, fh)] = evv
                    bank_free[bk].append(evv)
                    if e_i == NE - 1:
                        V(lambda e, tc=tc, fh=fh: e.bn_stats(
                            sttc[:, tc * 12 + 6 * fh: tc * 12 + 6 * fh + 6],
                            Fb[:, tc * 1024 + 512 * fh: tc * 1024 + 512 * (fh + 1)]), drain=True)
                        if fh == 1:
                            V(lambda e, tc=tc: e.bn_aggr(st2c[:, tc * 2:tc * 2 + 2],
                                                         sttc[:, tc * 12:tc * 12 + 12]), drain=True)
                consume_piece(key, ev_pl)

        ck("O")
        ev_st = V(lambda e: e.engine_nop(), inc=True)
        Sc(lambda e: e.activation(smc[:, 4:8], st2c[:, 1:8:2], AF.Ln, bias=epsc[:, 0:1]),
           waits=[ev_st])
        ev_r5 = Sc(lambda e: e.activation(smc[:, 8:12], smc[:, 4:8], AF.Exp, scale=-0.5),
                   inc=True, drain=True)
        ev_out = None
        for tc in range(TC):
            V(lambda e, tc=tc: e.tensor_scalar(
                Fc[:, tc * 1024:(tc + 1) * 1024], Fb[:, tc * 1024:(tc + 1) * 1024],
                st2c[:, 2 * tc:2 * tc + 1], smc[:, 8 + tc:9 + tc],
                ALU.subtract, ALU.mult), waits=[ev_r5] if tc == 0 else (), drain=(tc == 0))
        for tc in range(TC):
            ev_out = V(lambda e, tc=tc: e.tensor_tensor(
                Fc[:, tc * 1024:(tc + 1) * 1024], Fc[:, tc * 1024:(tc + 1) * 1024],
                Fa[:, tc * 1024:(tc + 1) * 1024], ALU.add), inc=(tc == TC - 1),
                drain=(tc == 0))
        ev_store = G(lambda e, tt=t: e.dma_start(
            dy.ap()[tt * TT:(tt + 1) * TT, :].rearrange("(tc p) f -> p tc f", p=128),
            Fc[:, :].rearrange("p (tc f) -> p tc f", f=D)),
            waits=[ev_out], qsem="qst")
        ev_fa_free = ev_out
        if t + 1 < NT:
            AB_stash = emit_AB(ev_tnorm_next, first=False)

    except _StopEmit:
        pass

    if CNT["qst"] == 0:
        ev_dummy = G(lambda e: e.dma_start(
            dy.ap()[0:TT, :].rearrange("(tc p) f -> p tc f", p=128),
            Fx0[:, :].rearrange("p (tc f) -> p tc f", f=D)), qsem="qst")
    G(lambda e: e.wait_ge(sem["qst"], CNT["qst"]))

    block = ctx.enter_context(nc.Block())

    @block.sync
    def _(q):
        for f in prog["Q"]:
            f(q)

    @block.gpsimd
    def _(g):
        for f in prog["G"]:
            f(g)

    @block.tensor
    def _(tt):
        for f in prog["T"]:
            f(tt)

    @block.vector
    def _(v):
        for f in prog["V"]:
            f(v)

    @block.scalar
    def _(s):
        for f in prog["S"]:
            f(s)

    ctx.close()
    return nc


def make_inputs(full, b, S):
    f = full
    im = {
        "x": np.ascontiguousarray(f["L0_attn_x"][b, :S, :]),
        "gate_w": f["L0_attn_gate_w"], "state_w": f["L0_attn_state_w"],
        "out_w": f["L0_attn_out_w"], "mlp_w1": f["L0_attn_mlp_w1"],
        "mlp_w2": f["L0_attn_mlp_w2"], "sw1": f["L0_ffn_sw1"], "sw2": f["L0_ffn_sw2"],
        "router_w": f["L0_ffn_router_w"], "rw1": f["L0_ffn_rw1"], "rw2": f["L0_ffn_rw2"],
        "norm1_w": f["L0_attn_norm1_w"].reshape(1, D), "norm1_b": f["L0_attn_norm1_b"].reshape(1, D),
        "norm2_w": f["L0_attn_norm2_w"].reshape(1, D), "norm2_b": f["L0_attn_norm2_b"].reshape(1, D),
        "gate_b": f["L0_attn_gate_b"].reshape(1, 2 * D), "state_b": f["L0_attn_state_b"].reshape(1, D),
        "out_b": f["L0_attn_out_b"].reshape(1, D), "mlp_b1": f["L0_attn_mlp_b1"].reshape(1, HM),
        "mlp_b2": f["L0_attn_mlp_b2"].reshape(1, D), "router_b": f["L0_ffn_router_b"].reshape(1, NE),
        "sb1": f["L0_ffn_sb1"].reshape(1, HF), "sb2": f["L0_ffn_sb2"].reshape(1, D),
        "rb1": f["L0_ffn_rb1"], "rb2": f["L0_ffn_rb2"].reshape(1, NE * D),
        "state_b_c": f["L0_attn_state_b"].reshape(8, 128).T,
        "gate_b_c": f["L0_attn_gate_b"].reshape(16, 128).T,
        "mlp_b1_c": f["L0_attn_mlp_b1"].reshape(32, 128).T,
        "sb1_c": f["L0_ffn_sb1"].reshape(16, 128).T,
        "rb1_c": f["L0_ffn_rb1"].reshape(NE, 16, 128).transpose(2, 0, 1).reshape(128, 128),
        "n1w_c": f["L0_attn_norm1_w"].reshape(8, 128).T,
        "n1b_c": f["L0_attn_norm1_b"].reshape(8, 128).T,
        "n2w_c": f["L0_attn_norm2_w"].reshape(8, 128).T,
        "n2b_c": f["L0_attn_norm2_b"].reshape(8, 128).T,
        "ident_f32": np.eye(128, dtype=np.float32),
        "ones_f": np.ones((1, 128), np.float32),
    }
    import ml_dtypes
    im["ones_b"] = np.ones((1, 128), ml_dtypes.bfloat16)
    im = {k: np.ascontiguousarray(v) for k, v in im.items()}
    return im


def kernel(**inputs):
    S = inputs["L0_attn_x"].shape[1]
    B = inputs["L0_attn_x"].shape[0]
    nc = build(S)
    in_maps = [make_inputs(inputs, b, S) for b in range(B)]
    res = run_bass_kernel_spmd(nc, in_maps, core_ids=list(range(B)))
    out = np.stack([res.results[b]["y"] for b in range(B)], axis=0)
    return out.astype(np.float32)


# version 07
# speedup vs baseline: 1.2335x; 1.0845x over previous
import numpy as np
from contextlib import ExitStack

import concourse.bass as bass
import concourse.mybir as mybir
from concourse.bass_utils import run_bass_kernel_spmd

F32 = mybir.dt.float32
BF16 = mybir.dt.bfloat16
AF = mybir.ActivationFunctionType
ALU = mybir.AluOpType
AX = mybir.AxisListType

D = 1024
DC = 8
TT = 512
TC = 4
HM = 4096
HF = 2048
NE = 8
EPS = 1e-5
SLOT_PP = 8192
NSLOT = 3
STOP_AFTER = None


class _StopEmit(Exception):
    pass


def _pieces_manifest():
    man = []
    man.append(("gateA", "gate_w", 0, D, 0, 1024))
    man.append(("gateB", "gate_w", 0, D, 1024, 1024))
    man.append(("state", "state_w", 0, D, 0, 1024))
    man.append(("outw", "out_w", 0, D, 0, 1024))
    for q in range(4):
        man.append((f"mlp1_{q}", "mlp_w1", 0, D, q * 1024, 1024))
    for fh in range(2):
        for kh in range(2):
            man.append((f"mlp2_{kh}{fh}", "mlp_w2", kh * 2048, 2048, fh * 512, 512))
    man.append(("sw1A", "sw1", 0, D, 0, 1024))
    man.append(("sw1B", "sw1", 0, D, 1024, 1024))
    for fh in range(2):
        man.append((f"sw2_{fh}", "sw2", 0, HF, fh * 512, 512))
    for e in range(NE):
        man.append((f"rw1A_{e}", ("rw1", e), 0, D, 0, 1024))
        man.append((f"rw1B_{e}", ("rw1", e), 0, D, 1024, 1024))
        for fh in range(2):
            man.append((f"rw2_{fh}_{e}", ("rw2", e), 0, HF, fh * 512, 512))
    return man


def build(S):
    NT = S // TT
    nc = bass.Bass("TRN2", target_bir_lowering=False)

    dx = nc.dram_tensor("x", [S, D], F32, kind="ExternalInput")
    dy = nc.dram_tensor("y", [S, D], F32, kind="ExternalOutput")
    dw = {}
    for nm, shp in [
        ("gate_w", [D, 2 * D]), ("state_w", [D, D]), ("out_w", [D, D]),
        ("mlp_w1", [D, HM]), ("mlp_w2", [HM, D]),
        ("sw1", [D, HF]), ("sw2", [HF, D]), ("router_w", [D, NE]),
        ("rw1", [NE, D, HF]), ("rw2", [NE, HF, D]),
        ("norm1_w", [1, D]), ("norm1_b", [1, D]), ("norm2_w", [1, D]), ("norm2_b", [1, D]),
        ("gate_b", [1, 2 * D]), ("state_b", [1, D]), ("out_b", [1, D]),
        ("mlp_b1", [1, HM]), ("mlp_b2", [1, D]), ("router_b", [1, NE]),
        ("sb1", [1, HF]), ("sb2", [1, D]), ("rb1", [NE, HF]), ("rb2", [1, NE * D]),
        ("state_b_c", [128, 8]), ("gate_b_c", [128, 16]), ("mlp_b1_c", [128, 32]),
        ("sb1_c", [128, 16]), ("rb1_c", [128, 128]),
        ("n1w_c", [128, 8]), ("n1b_c", [128, 8]), ("n2w_c", [128, 8]), ("n2b_c", [128, 8]),
    ]:
        dw[nm] = nc.dram_tensor(nm, shp, F32, kind="ExternalInput")
    d_id = nc.dram_tensor("ident_f32", [128, 128], F32, kind="ExternalInput")
    d_ones_f = nc.dram_tensor("ones_f", [1, 128], F32, kind="ExternalInput")
    d_ones_b = nc.dram_tensor("ones_b", [1, 128], BF16, kind="ExternalInput")

    man = _pieces_manifest()
    NPIECE = len(man)
    wcache = nc.dram_tensor("wcache", [NPIECE, 128, SLOT_PP], BF16, kind="Internal")

    ctx = ExitStack()
    sb = lambda n, s, d=F32: ctx.enter_context(nc.sbuf_tensor(n, s, d))

    Fx0 = sb("Fx0", [128, 4096])
    Ftn = sb("Ftn", [128, 4096])
    Fa = sb("Fa", [128, 4096])
    Fb = sb("Fb", [128, 4096])
    Fc = sb("Fc", [128, 4096])
    Bxn = sb("Bxn", [128, 4096], BF16)
    Bxg = sb("Bxg", [128, 4096], BF16)
    Bgg = sb("Bgg", [128, 4096], BF16)
    Bys = sb("Bys", [128, 4096], BF16)
    Bh0 = sb("Bh0", [128, 8192], BF16)
    WS = [ctx.enter_context(nc.sbuf_tensor(f"WS{i}", [128, SLOT_PP], BF16)) for i in range(NSLOT)]
    ident = sb("ident", [128, 128])
    ones_f = sb("ones_fs", [1, 128])
    ones_b = sb("ones_bs", [1, 128], BF16)
    rowsb = sb("rowsb", [1, 4096], BF16)
    rb2all = sb("rb2all", [8, 1024], BF16)
    w2kT = sb("w2kT", [8, 512], BF16)
    routw = sb("routw", [128, 64], BF16)
    colsb = sb("colsb", [128, 256])
    CB_STATE, CB_GATE, CB_MLP1, CB_SB1, CB_RB1 = 0, 8, 24, 56, 72
    CB_N1W, CB_N1B, CB_N2W, CB_N2B = 200, 208, 216, 224
    bbar = sb("bbar", [1, 1])
    epsc = sb("epsc", [128, 1])
    wbar = sb("wbar", [128, 8], BF16)
    wbarf = sb("wbarf", [128, 8])
    drA = sb("drA", [1, 512])
    drB = sb("drB", [1, 512])
    drep = sb("drep", [128, 512])
    darep = sb("darep", [128, 512])
    hcol = sb("hcol", [128, 16])
    stt = sb("stt", [128, 48])
    sttb = sb("sttb", [128, 48])
    st2b = sb("st2b", [128, 8])
    smcb = sb("smcb", [128, 24])
    st2 = sb("st2", [128, 8])
    sttc = sb("sttc", [128, 48])
    st2c = sb("st2c", [128, 8])
    smc = sb("smc", [128, 40])
    w2k = sb("w2k", [128, 32])
    pxp = sb("pxp", [128, 32])

    def hup(c, o0, o1):
        buf = Bys if c < 8 else Bxg
        b = (c % 8) * 512
        return buf[:, b + o0: b + o1]

    psum = ctx.enter_context(nc.psum_tensor("pall", [128, 4096], F32))
    BK = lambda i: psum[:, 512 * i:512 * (i + 1)]

    sem = {}
    for s in ["sT", "sV", "sS", "qw0", "qw1", "qw2", "qc0", "qc1", "qc2", "qx", "qst", "qg0", "qg1", "qg2", "qb", "qgb"]:
        sem[s] = ctx.enter_context(nc.semaphore(s))
    CNT = {k: 0 for k in sem}
    prog = {k: [] for k in ["T", "V", "S", "Q", "G"]}
    need_drain = {"V": False, "S": False}

    def emit(E, fn, waits=(), inc=None, drain=False):
        ws = [w for w in waits if w is not None]
        if drain:
            prog[E].append(lambda e: e.drain())
        val = None
        if inc is not None:
            iname, iamt = inc
            CNT[iname] += iamt
            val = (iname, CNT[iname])

        def g(e, _ws=tuple(ws), _fn=fn, _inc=inc):
            for s, v in _ws:
                e.wait_ge(sem[s], v)
            inst = _fn(e)
            if _inc is not None:
                inst.then_inc(sem[_inc[0]], _inc[1])
        prog[E].append(g)
        return val

    def T(fn, waits=(), inc=False):
        return emit("T", fn, waits, ("sT", 1) if inc else None)

    def V(fn, waits=(), inc=False, drain=False):
        return emit("V", fn, waits, ("sV", 1) if inc else None, drain=drain)

    def Sc(fn, waits=(), inc=False, drain=False):
        return emit("S", fn, waits, ("sS", 1) if inc else None, drain=drain)

    def Q(fn, waits=(), qsem=None):
        return emit("Q", fn, waits, (qsem, 16) if qsem else None)

    def G(fn, waits=(), qsem=None):
        return emit("G", fn, waits, (qsem, 16) if qsem else None)

    bank_free = [[] for _ in range(8)]

    def use_bank(i):
        evs = bank_free[i]
        bank_free[i] = []
        return evs

    Q(lambda e: e.dma_start(ident[:, :], d_id.ap()), qsem="qb")
    Q(lambda e: e.dma_start(ones_f[:, :], d_ones_f.ap()), qsem="qb")
    Q(lambda e: e.dma_start(ones_b[:, :], d_ones_b.ap()), qsem="qb")
    Q(lambda e: e.dma_start(Ftn[0:1, 0:D], dw["state_b"].ap()), qsem="qb")

    def col_load(dst_c0, src_name, n_chunks):
        Q(lambda e, a=dw[src_name], c0=dst_c0, nch=n_chunks: e.dma_start(
            colsb[:, c0:c0 + nch], a.ap()), qsem="qb")

    col_load(CB_STATE, "state_b_c", 8)
    col_load(CB_GATE, "gate_b_c", 16)
    col_load(CB_MLP1, "mlp_b1_c", 32)
    col_load(CB_SB1, "sb1_c", 16)
    col_load(CB_RB1, "rb1_c", 128)
    col_load(CB_N1W, "n1w_c", 8)
    col_load(CB_N1B, "n1b_c", 8)
    col_load(CB_N2W, "n2w_c", 8)
    col_load(CB_N2B, "n2b_c", 8)
    ev_qb = ("qb", CNT["qb"])

    G(lambda e: e.dma_start(rowsb[:, 0:1024], dw["out_b"].ap()), qsem="qgb")
    G(lambda e: e.dma_start(rowsb[:, 1024:2048], dw["mlp_b2"].ap()), qsem="qgb")
    G(lambda e: e.dma_start(rowsb[:, 2048:3072], dw["sb2"].ap()), qsem="qgb")
    G(lambda e: e.dma_start(rowsb[:, 3072:3072 + NE], dw["router_b"].ap()), qsem="qgb")
    G(lambda e: e.dma_start(rb2all[:, :],
                            dw["rb2"].ap()[0:1, :].rearrange("o (e f) -> (o e) f", f=D)), qsem="qgb")
    G(lambda e: e.dma_start(
        routw[:, :].rearrange("p (c j) -> p c j", j=NE),
        dw["router_w"].ap().rearrange("(c p) j -> p c j", p=128),
        allow_slow_non_contiguous=True), qsem="qgb")
    ev_qgb = ("qgb", CNT["qgb"])

    ev_x = G(lambda e: e.dma_start(
        Fx0[:, :].rearrange("p (tc f) -> p tc f", f=D),
        dx.ap()[0:TT, :].rearrange("(tc p) f -> p tc f", p=128)), qsem="qx")

    V(lambda e: e.memset(epsc[:, :], EPS))
    V(lambda e: e.reduce_sum(bbar[:, :], Ftn[0:1, 0:D], axis=AX.X), waits=[ev_qb])
    ev_bbar_pre = V(lambda e: e.tensor_scalar(bbar[:, :], bbar[:, :], 1.0 / D, None, ALU.mult),
                    inc=True, drain=True)

    piece_src = {}
    for i, (key, src, k0, K, n0, N) in enumerate(man):
        piece_src[key] = (i, src, k0, K, n0, N)

    slot_free = [[] for _ in range(NSLOT)]
    slot_rr = [0]
    piece_ev = {}
    cache_store_ev = {}

    def load_piece(t, key):
        idx, src, k0, K, n0, N = piece_src[key]
        nch = K // 128
        npp = nch * N
        sl = slot_rr[0]
        slot_rr[0] = (sl + 1) % NSLOT
        dst = WS[sl][:, :npp].rearrange("p (c n) -> p c n", n=N)
        if isinstance(src, tuple):
            srct = dw[src[0]].ap()[src[1]]
        else:
            srct = dw[src].ap()
        srcap = srct[k0:k0 + K, n0:n0 + N].rearrange("(c p) n -> p c n", p=128)
        frees = slot_free[sl]
        slot_free[sl] = []
        if t == 0:
            ev = G(lambda e, d=dst, s=srcap: e.dma_start(d, s),
                   waits=frees, qsem=f"qg{sl}")
            sev = Q(lambda e, i2=idx, npp2=npp, sl2=sl: e.dma_start(
                wcache.ap()[i2][:, :npp2], WS[sl2][:, :npp2]),
                waits=[ev], qsem=f"qc{sl}")
            cache_store_ev[idx] = sev
            slot_free[sl].append(sev)
        else:
            ws = list(frees)
            if t == 1:
                ws.append(cache_store_ev[idx])
            ev = Q(lambda e, d=dst, s=wcache.ap()[idx][:, :npp]: e.dma_start(d, s),
                   waits=ws, qsem=f"qw{sl}")
        piece_ev[key] = (sl, ev)
        return sl, ev

    def consume_piece(key, tev):
        sl, _ = piece_ev[key]
        slot_free[sl].append(tev)

    ev_store = None
    wbar_ready = None

    piece_index = {m[0]: i for i, m in enumerate(man)}

    def ck(name):
        if STOP_AFTER is not None and name == STOP_AFTER:
            raise _StopEmit()

    def emit_A_stats(wait_ev):
        for tc in range(TC):
            for h in range(2):
                V(lambda e, tc=tc, h=h: e.bn_stats(
                    sttb[:, tc * 12 + 6 * h: tc * 12 + 6 * h + 6],
                    Fx0[:, tc * 1024 + 512 * h: tc * 1024 + 512 * (h + 1)]),
                  waits=[wait_ev] if (tc == 0 and h == 0) else (),
                  drain=(tc == 0 and h == 0))
        for tc in range(TC):
            V(lambda e, tc=tc: e.bn_aggr(st2b[:, tc * 2:tc * 2 + 2], sttb[:, tc * 12:tc * 12 + 12]),
              drain=(tc == 0))
        ev_st = V(lambda e: e.tensor_copy(smcb[:, 0:4], st2b[:, 1:8:2]), inc=True, drain=True)
        Sc(lambda e: e.activation(smcb[:, 4:8], smcb[:, 0:4], AF.Ln, bias=epsc[:, 0:1]), waits=[ev_st])
        ev_r0 = Sc(lambda e: e.activation(smcb[:, 8:12], smcb[:, 4:8], AF.Exp, scale=-0.5),
                   inc=True, drain=True)
        V(lambda e: e.tensor_tensor(smcb[:, 12:16], smcb[:, 8:12], smcb[:, 8:12], ALU.mult),
          waits=[ev_r0], drain=True)
        ev_v1 = V(lambda e: e.tensor_tensor(smcb[:, 12:16], smcb[:, 12:16], smcb[:, 0:4], ALU.mult),
                  inc=True, drain=True)
        Sc(lambda e: e.activation(smcb[:, 16:20], smcb[:, 12:16], AF.Ln, bias=epsc[:, 0:1]),
           waits=[ev_v1], drain=True)
        ev_r1 = Sc(lambda e: e.activation(smcb[:, 16:20], smcb[:, 16:20], AF.Exp, scale=-0.5),
                   inc=True, drain=True)
        return V(lambda e: e.tensor_tensor(smcb[:, 20:24], smcb[:, 8:12], smcb[:, 16:20], ALU.mult),
                 waits=[ev_r1], inc=True, drain=True)

    def emit_Anorm(waits):
        ev = None
        for tc in range(TC):
            ev = V(lambda e, tc=tc: e.tensor_scalar(
                Ftn[:, tc * 1024:(tc + 1) * 1024], Fx0[:, tc * 1024:(tc + 1) * 1024],
                st2b[:, 2 * tc:2 * tc + 1], smcb[:, 20 + tc:21 + tc],
                ALU.subtract, ALU.mult), waits=waits if tc == 0 else (), inc=(tc == TC - 1),
                drain=(tc == 0))
        return ev

    def emit_AB(ev_tnorm, first):
        ev_trs = {}
        for fc in range(DC):
            wt = use_bank(fc) + ([ev_tnorm] if fc == 0 else [])
            if first and fc == 0:
                wt += [ev_qb, ev_qgb]
            for tcj in range(TC):
                ev_trs[fc] = T(lambda e, fc=fc, tcj=tcj: e.transpose(
                    BK(fc)[:, tcj * 128:(tcj + 1) * 128],
                    Ftn[:, tcj * 1024 + fc * 128: tcj * 1024 + (fc + 1) * 128],
                    ident[:, :]), waits=wt if tcj == 0 else (), inc=(tcj == TC - 1))
        ev_xnf = {}
        ev_xnb = None
        for fc in range(DC):
            ev_xnf[fc] = Sc(lambda e, fc=fc: e.activation(
                Ftn[:, fc * 512:(fc + 1) * 512], BK(fc), AF.Identity,
                scale=colsb[:, CB_N1W + fc:CB_N1W + fc + 1],
                bias=colsb[:, CB_N1B + fc:CB_N1B + fc + 1]),
                waits=[ev_trs[DC - 1]] if fc == 0 else (), inc=True)
            bank_free[fc].append(ev_xnf[fc])
            ev_xnb = V(lambda e, fc=fc: e.tensor_copy(
                Bxn[:, fc * 512:(fc + 1) * 512], Ftn[:, fc * 512:(fc + 1) * 512]),
                waits=[ev_xnf[fc]], inc=True)
        return ev_xnb, ev_xnf

    emit_A_stats(ev_x)
    AB_stash = None
    ev_fa_free = None

    try:
      for t in range(NT):
        par = t % 2
        next_pi = [0]

        def ensure(key, ahead=2, _t=t, _next=None):
            target = min(piece_index[key] + ahead, NPIECE - 1)
            while next_pi[0] <= target:
                load_piece(t, man[next_pi[0]][0])
                next_pi[0] += 1

        if t == 0:
            ev_tnorm0 = emit_Anorm([])
            ev_xnb, ev_xnf = emit_AB(ev_tnorm0, first=True)
        else:
            ev_xnb, ev_xnf = AB_stash

        ck("B")
        ev_xg = {}
        ev_gg = {}
        for half, key in enumerate(["gateA", "gateB"]):
            ensure(key)
            slg, evg = piece_ev[key]
            ev_last_mm = None
            for fo in range(8):
                bk = fo % 4
                wt = use_bank(bk) + ([evg, ev_xnb] if fo in (0, 1, 2, 3) else [])
                for kc in range(DC):
                    ev_last_mm = T(lambda e, bk=bk, slg=slg, kc=kc, fo=fo: e.matmul(
                        BK(bk), WS[slg][:, kc * 1024 + fo * 128: kc * 1024 + (fo + 1) * 128],
                        Bxn[:, kc * 512:(kc + 1) * 512],
                        start=(kc == 0), stop=(kc == DC - 1)),
                        waits=wt if kc == 0 else (), inc=(kc == DC - 1))
                if half == 0:
                    evs = Sc(lambda e, bk=bk, fo=fo: e.activation(
                        BK(bk), BK(bk), AF.Silu,
                        bias=colsb[:, CB_GATE + fo:CB_GATE + fo + 1]),
                        waits=[ev_last_mm], inc=True)
                    evv = V(lambda e, bk=bk, fo=fo: e.tensor_tensor(
                        Bxg[:, fo * 512:(fo + 1) * 512], BK(bk),
                        Ftn[:, fo * 512:(fo + 1) * 512], ALU.mult),
                        waits=[evs], inc=True)
                    ev_xg[fo] = evv
                    bank_free[bk].append(evv)
                else:
                    evs = Sc(lambda e, bk=bk, fo=fo: e.activation(
                        Bgg[:, fo * 512:(fo + 1) * 512], BK(bk), AF.Silu,
                        bias=colsb[:, CB_GATE + 8 + fo:CB_GATE + 8 + fo + 1]),
                        waits=[ev_last_mm], inc=True)
                    ev_gg[fo] = evs
                    bank_free[bk].append(evs)
            consume_piece(key, ev_last_mm)

        ck("C")
        ensure("state")
        sls, evsl = piece_ev["state"]
        ev_u = {}
        ev_bm = {}
        ev_last_state_mm = None
        for fo in range(8):
            for which in range(2):
                bk = (2 * fo + which) % 4
                rhs_buf = Bxg if which == 0 else Bgg
                rdy = ev_xg[7] if which == 0 else ev_gg[7]
                wt = use_bank(bk) + ([evsl, rdy] if fo == 0 else [])
                for kc in range(DC):
                    ev_last_state_mm = T(lambda e, bk=bk, sls=sls, kc=kc, fo=fo, rb=rhs_buf: e.matmul(
                        BK(bk), WS[sls][:, kc * 1024 + fo * 128: kc * 1024 + (fo + 1) * 128],
                        rb[:, kc * 512:(kc + 1) * 512],
                        start=(kc == 0), stop=(kc == DC - 1)),
                        waits=wt if kc == 0 else (), inc=(kc == DC - 1))
                dstb = Fa if which == 0 else Fb
                ew = [ev_last_state_mm]
                if which == 0 and fo == 0:
                    ew.append(ev_fa_free)
                evs = Sc(lambda e, bk=bk, fo=fo, dstb=dstb: e.activation(
                    dstb[:, fo * 512:(fo + 1) * 512], BK(bk), AF.Identity,
                    bias=colsb[:, CB_STATE + fo:CB_STATE + fo + 1]),
                    waits=ew, inc=True)
                bank_free[bk].append(evs)
                if which == 0:
                    ev_u[fo] = evs
                else:
                    ev_bm[fo] = evs
        consume_piece("state", ev_last_state_mm)

        ck("E")
        if t == 0:
            slst = sls
            for kc in range(DC):
                V(lambda e, kc=kc, s=slst: e.reduce_sum(
                    wbarf[:, kc:kc + 1], WS[s][:, kc * 1024:(kc + 1) * 1024], axis=AX.X),
                  waits=[evsl] if kc == 0 else ())
            V(lambda e: e.tensor_scalar(wbarf[:, :], wbarf[:, :], 1.0 / D, None, ALU.mult), drain=True)
            wbar_ready = V(lambda e: e.tensor_copy(wbar[:, :], wbarf[:, :]), inc=True, drain=True)
            slot_free[sls].append(wbar_ready)

        wt = use_bank(6) + [ev_xnb, wbar_ready]
        ev_dmm = None
        for kc in range(DC):
            ev_dmm = T(lambda e, kc=kc: e.matmul(
                BK(6)[0:1, :], wbar[:, kc:kc + 1], Bxn[:, kc * 512:(kc + 1) * 512],
                start=(kc == 0), stop=(kc == DC - 1)),
                waits=wt if kc == 0 else (), inc=(kc == DC - 1))
        Sc(lambda e: e.activation(drA[0:1, :], BK(6)[0:1, :], AF.Exp, bias=bbar[0:1, 0:1]),
           waits=[ev_dmm, ev_bbar_pre])
        ev_dsp = Sc(lambda e: e.activation(drB[0:1, :], drA[0:1, :], AF.Ln, bias=1.0),
                    inc=True, drain=True)
        ev_dA = Sc(lambda e: e.activation(drA[0:1, :], drB[0:1, :], AF.Exp, scale=-1.0),
                   inc=True, drain=True)
        bank_free[6].append(ev_dsp)
        wt6 = use_bank(6)
        ev_rp1 = T(lambda e: e.matmul(BK(6), ones_f[0:1, :], drB[0:1, :], start=True, stop=True),
                   waits=wt6 + [ev_dsp], inc=True)
        wt7 = use_bank(7)
        ev_rp2 = T(lambda e: e.matmul(BK(7), ones_f[0:1, :], drA[0:1, :], start=True, stop=True),
                   waits=wt7 + [ev_dA], inc=True)
        ev_drep = V(lambda e: e.tensor_copy(drep[:, :], BK(6)), waits=[ev_rp1], inc=True)
        ev_darep = V(lambda e: e.tensor_copy(darep[:, :], BK(7)), waits=[ev_rp2], inc=True)
        bank_free[6].append(ev_drep)
        bank_free[7].append(ev_darep)

        ck("D")
        for fo in range(8):
            wdep = [ev_u[fo], ev_bm[fo]]
            if ev_store is not None and fo == 0:
                wdep.append(ev_store)
            V(lambda e, fo=fo: e.tensor_tensor(
                Fc[:, fo * 512:(fo + 1) * 512], Fa[:, fo * 512:(fo + 1) * 512],
                Fb[:, fo * 512:(fo + 1) * 512], ALU.mult), waits=wdep, drain=(fo == 0))
        ev_dbu = {}
        for fo in range(8):
            ev_dbu[fo] = V(lambda e, fo=fo: e.tensor_tensor(
                Fc[:, fo * 512:(fo + 1) * 512], Fc[:, fo * 512:(fo + 1) * 512],
                drep[:, :], ALU.mult), waits=[ev_drep] if fo == 0 else (),
                inc=True, drain=(fo == 0))

        ev_scan = {}
        for fo in range(8):
            if t == 0:
                init = 0.0
            else:
                init = hcol[:, (1 - par) * 8 + fo:(1 - par) * 8 + fo + 1]
            ev_scan[fo] = V(lambda e, fo=fo, init=init: e.tensor_tensor_scan(
                Bys[:, fo * 512:(fo + 1) * 512], darep[:, :],
                Fc[:, fo * 512:(fo + 1) * 512], init, ALU.mult, ALU.add),
                waits=[ev_darep] if fo == 0 else (), inc=True, drain=(fo == 0))
        for fo in range(8):
            V(lambda e, fo=fo, par=par: e.tensor_copy(
                hcol[:, par * 8 + fo:par * 8 + fo + 1],
                Bys[:, fo * 512 + 511:fo * 512 + 512]), drain=(fo == 0))

        ck("F")
        ensure("outw")
        slo, evo = piece_ev["outw"]
        ev_mid = {}
        ev_last_out_mm = None
        ev_xin = None
        for tc in range(TC):
            ev_xin = V(lambda e, tc=tc: e.tensor_scalar(
                Ftn[:, tc * 1024:(tc + 1) * 1024], Fx0[:, tc * 1024:(tc + 1) * 1024],
                st2b[:, 2 * tc:2 * tc + 1], smcb[:, 8 + tc:9 + tc],
                ALU.subtract, ALU.mult),
                waits=[ev_xnf[DC - 1]] if tc == 0 else (), inc=(tc == TC - 1),
                drain=(tc == 0))
        for tc in range(TC):
            for nh in range(2):
                bk = 2 + ((2 * tc + nh) % 4)
                wt = use_bank(bk) + ([evo, ev_scan[7]] if tc == 0 else [])
                for kc in range(DC):
                    T(lambda e, bk=bk, slo=slo, kc=kc, tc=tc, nh=nh: e.matmul(
                        BK(bk), Bys[:, kc * 512 + tc * 128: kc * 512 + (tc + 1) * 128],
                        WS[slo][:, kc * 1024 + nh * 512: kc * 1024 + (nh + 1) * 512],
                        start=(kc == 0), stop=False),
                        waits=wt if kc == 0 else ())
                ev_last_out_mm = T(lambda e, bk=bk, nh=nh: e.matmul(
                    BK(bk), ones_b[0:1, :], rowsb[0:1, nh * 512:(nh + 1) * 512],
                    start=False, stop=True), inc=True)
                evv = V(lambda e, bk=bk, tc=tc, nh=nh: e.tensor_tensor(
                    Fc[:, tc * 1024 + nh * 512: tc * 1024 + (nh + 1) * 512], BK(bk),
                    Ftn[:, tc * 1024 + nh * 512: tc * 1024 + (nh + 1) * 512], ALU.add),
                    waits=[ev_last_out_mm, ev_xin] if (tc == 0 and nh == 0) else [ev_last_out_mm],
                    inc=True, drain=(tc == 0 and nh == 0))
                ev_mid[(tc, nh)] = evv
                bank_free[bk].append(evv)
                V(lambda e, tc=tc, nh=nh: e.bn_stats(
                    stt[:, tc * 12 + 6 * nh: tc * 12 + 6 * nh + 6],
                    Fc[:, tc * 1024 + 512 * nh: tc * 1024 + 512 * (nh + 1)]), drain=True)
                if nh == 1:
                    V(lambda e, tc=tc: e.bn_aggr(st2[:, tc * 2:tc * 2 + 2],
                                                 stt[:, tc * 12:tc * 12 + 12]), drain=True)
        consume_piece("outw", ev_last_out_mm)
        ev_mid_all = ev_mid[(TC - 1, 1)]

        ck("G")
        ev_st = V(lambda e: e.engine_nop(), inc=True)
        Sc(lambda e: e.activation(smc[:, 4:8], st2[:, 1:8:2], AF.Ln, bias=epsc[:, 0:1]),
           waits=[ev_st])
        ev_r2 = Sc(lambda e: e.activation(smc[:, 8:12], smc[:, 4:8], AF.Exp, scale=-0.5),
                   inc=True, drain=True)
        ev_tnorm2 = None
        for tc in range(TC):
            ev_tnorm2 = V(lambda e, tc=tc: e.tensor_scalar(
                Ftn[:, tc * 1024:(tc + 1) * 1024], Fc[:, tc * 1024:(tc + 1) * 1024],
                st2[:, 2 * tc:2 * tc + 1], smc[:, 8 + tc:9 + tc],
                ALU.subtract, ALU.mult), waits=[ev_r2] if tc == 0 else (), inc=(tc == TC - 1),
                drain=(tc == 0))
        ev_x2 = {}
        for fc in range(DC):
            bk = fc
            wt = use_bank(bk) + ([ev_tnorm2] if fc in (0, 1) else [])
            for tcj in range(TC):
                ev_tr = T(lambda e, bk=bk, tcj=tcj, fc=fc: e.transpose(
                    BK(bk)[:, tcj * 128:(tcj + 1) * 128],
                    Ftn[:, tcj * 1024 + fc * 128: tcj * 1024 + (fc + 1) * 128],
                    ident[:, :]), waits=wt if tcj == 0 else (), inc=(tcj == TC - 1))
            ev_x2[fc] = Sc(lambda e, bk=bk, fc=fc: e.activation(
                Bxn[:, fc * 512:(fc + 1) * 512], BK(bk), AF.Identity,
                scale=colsb[:, CB_N2W + fc:CB_N2W + fc + 1],
                bias=colsb[:, CB_N2B + fc:CB_N2B + fc + 1]), waits=[ev_tr], inc=True)
            bank_free[bk].append(ev_x2[fc])
        ev_x2_all = ev_x2[DC - 1]

        ck("H")
        ev_h1 = {}
        for q in range(4):
            key = f"mlp1_{q}"
            ensure(key)
            slq, evq = piece_ev[key]
            ev_last = None
            for fl in range(8):
                hc = q * 8 + fl
                bk = hc % 4
                wt = use_bank(bk) + ([evq, ev_x2_all] if fl in (0, 1, 2, 3) else [])
                for kc in range(DC):
                    ev_last = T(lambda e, bk=bk, slq=slq, kc=kc, fl=fl: e.matmul(
                        BK(bk), WS[slq][:, kc * 1024 + fl * 128: kc * 1024 + (fl + 1) * 128],
                        Bxn[:, kc * 512:(kc + 1) * 512],
                        start=(kc == 0), stop=(kc == DC - 1)),
                        waits=wt if kc == 0 else (), inc=(kc == DC - 1))
                if hc < 16:
                    dap = Bh0[:, hc * 512:(hc + 1) * 512]
                else:
                    dap = hup(hc - 16, 0, 512)
                evs = Sc(lambda e, bk=bk, hc=hc, dap=dap: e.activation(
                    dap, BK(bk), AF.Gelu,
                    bias=colsb[:, CB_MLP1 + hc:CB_MLP1 + hc + 1]),
                    waits=[ev_last], inc=True)
                ev_h1[hc] = evs
                bank_free[bk].append(evs)
            consume_piece(key, ev_last)

        ck("I")
        ev_bout = {}
        for fh in range(2):
            for kh in range(2):
                key = f"mlp2_{kh}{fh}"
                ensure(key)
                slm, evm = piece_ev[key]
                ev_pl = None
                for tc in range(TC):
                    bk = 2 + tc
                    if kh == 0:
                        wt = use_bank(bk) + [evm, ev_h1[15]]
                    else:
                        wt = [evm, ev_h1[31]]
                    for kl in range(16):
                        lhs = (Bh0[:, kl * 512 + tc * 128: kl * 512 + (tc + 1) * 128]
                               if kh == 0 else hup(kl, tc * 128, (tc + 1) * 128))
                        ev_pl = T(lambda e, bk=bk, slm=slm, kl=kl, lhs=lhs,
                                  st=(kh == 0 and kl == 0): e.matmul(
                            BK(bk), lhs,
                            WS[slm][:, kl * 512:(kl + 1) * 512],
                            start=st, stop=False),
                            waits=wt if kl == 0 else (),
                            inc=(tc == TC - 1 and kl == 15))
                consume_piece(key, ev_pl)
            evb = None
            for tc in range(TC):
                bk = 2 + tc
                evb = T(lambda e, bk=bk, fh=fh: e.matmul(
                    BK(bk), ones_b[0:1, :], rowsb[0:1, 1024 + fh * 512:1024 + (fh + 1) * 512],
                    start=False, stop=True), inc=True)
                evv = V(lambda e, bk=bk, tc=tc, fh=fh: e.tensor_tensor(
                    Fc[:, tc * 1024 + fh * 512: tc * 1024 + (fh + 1) * 512], BK(bk),
                    Fc[:, tc * 1024 + fh * 512: tc * 1024 + (fh + 1) * 512], ALU.add),
                    waits=[evb], inc=True, drain=(tc == 0 and fh == 0))
                ev_bout[(tc, fh)] = evv
                bank_free[bk].append(evv)
                V(lambda e, tc=tc, fh=fh: e.bn_stats(
                    sttc[:, tc * 12 + 6 * fh: tc * 12 + 6 * fh + 6],
                    Fc[:, tc * 1024 + 512 * fh: tc * 1024 + 512 * (fh + 1)]), drain=True)
                if fh == 1:
                    V(lambda e, tc=tc: e.bn_aggr(st2c[:, tc * 2:tc * 2 + 2],
                                                 sttc[:, tc * 12:tc * 12 + 12]), drain=True)

        ck("J")
        ev_st = V(lambda e: e.engine_nop(), inc=True)
        Sc(lambda e: e.activation(smc[:, 4:8], st2c[:, 1:8:2], AF.Ln, bias=epsc[:, 0:1]),
           waits=[ev_st])
        ev_r3 = Sc(lambda e: e.activation(smc[:, 8:12], smc[:, 4:8], AF.Exp, scale=-0.5),
                   inc=True, drain=True)
        ev_x1 = None
        for tc in range(TC):
            V(lambda e, tc=tc: e.tensor_scalar(
                Ftn[:, tc * 1024:(tc + 1) * 1024], Fc[:, tc * 1024:(tc + 1) * 1024],
                st2c[:, 2 * tc:2 * tc + 1], smc[:, 8 + tc:9 + tc],
                ALU.subtract, ALU.mult), waits=[ev_r3] if tc == 0 else (), drain=(tc == 0))
        for tc in range(TC):
            ev_x1 = V(lambda e, tc=tc: e.tensor_tensor(
                Fa[:, tc * 1024:(tc + 1) * 1024], Ftn[:, tc * 1024:(tc + 1) * 1024],
                Fx0[:, tc * 1024:(tc + 1) * 1024], ALU.add), inc=(tc == TC - 1),
                drain=(tc == 0))
        for tc in range(TC):
            for h in range(2):
                V(lambda e, tc=tc, h=h: e.bn_stats(
                    stt[:, tc * 12 + 6 * h: tc * 12 + 6 * h + 6],
                    Fa[:, tc * 1024 + 512 * h: tc * 1024 + 512 * (h + 1)]),
                  drain=(tc == 0 and h == 0))
        for tc in range(TC):
            V(lambda e, tc=tc: e.bn_aggr(st2[:, tc * 2:tc * 2 + 2], stt[:, tc * 12:tc * 12 + 12]),
              drain=(tc == 0))
        if t + 1 < NT:
            ev_x = G(lambda e, tt=t + 1: e.dma_start(
                Fx0[:, :].rearrange("p (tc f) -> p tc f", f=D),
                dx.ap()[tt * TT:(tt + 1) * TT, :].rearrange("(tc p) f -> p tc f", p=128)),
                waits=[ev_x1], qsem="qx")
            emit_A_stats(ev_x)

        ck("K")
        ev_st = V(lambda e: e.engine_nop(), inc=True)
        Sc(lambda e: e.activation(smc[:, 4:8], st2[:, 1:8:2], AF.Ln, bias=epsc[:, 0:1]),
           waits=[ev_st])
        ev_r4 = Sc(lambda e: e.activation(smc[:, 8:12], smc[:, 4:8], AF.Exp, scale=-0.5),
                   inc=True, drain=True)
        ev_tnorm4 = None
        for tc in range(TC):
            ev_tnorm4 = V(lambda e, tc=tc: e.tensor_scalar(
                Ftn[:, tc * 1024:(tc + 1) * 1024], Fa[:, tc * 1024:(tc + 1) * 1024],
                st2[:, 2 * tc:2 * tc + 1], smc[:, 8 + tc:9 + tc],
                ALU.subtract, ALU.mult), waits=[ev_r4] if tc == 0 else (), inc=(tc == TC - 1),
                drain=(tc == 0))
        ev_fin = {}
        for fc in range(DC):
            bk = fc
            wt = use_bank(bk) + ([ev_tnorm4] if fc in (0, 1) else [])
            for tcj in range(TC):
                ev_tr = T(lambda e, bk=bk, tcj=tcj, fc=fc: e.transpose(
                    BK(bk)[:, tcj * 128:(tcj + 1) * 128],
                    Ftn[:, tcj * 1024 + fc * 128: tcj * 1024 + (fc + 1) * 128],
                    ident[:, :]), waits=wt if tcj == 0 else (), inc=(tcj == TC - 1))
            ev_fin[fc] = Sc(lambda e, bk=bk, fc=fc: e.activation(
                Bgg[:, fc * 512:(fc + 1) * 512], BK(bk), AF.Identity), waits=[ev_tr], inc=True)
            bank_free[bk].append(ev_fin[fc])
        ev_fin_all = ev_fin[DC - 1]
        ev_trL = ev_tr
        ev_tnorm_next = emit_Anorm([ev_trL]) if t + 1 < NT else None

        ck("L")
        ev_rt = {}
        ev_w2k = None
        for tc in range(TC):
            bk = 6 + (tc % 2)
            wt = use_bank(bk)
            if tc == 0:
                wt += [ev_fin_all, ev_qgb]
            for kc in range(DC):
                T(lambda e, tc=tc, kc=kc, bk=bk: e.matmul(
                    BK(bk)[:, tc * 8:tc * 8 + 8],
                    Bgg[:, kc * 512 + tc * 128: kc * 512 + (tc + 1) * 128],
                    routw[:, kc * 8:(kc + 1) * 8],
                    start=(kc == 0), stop=False),
                    waits=wt if kc == 0 else ())
            ev_rt[tc] = T(lambda e, tc=tc, bk=bk: e.matmul(
                BK(bk)[:, tc * 8:tc * 8 + 8], ones_b[0:1, :], rowsb[0:1, 3072:3072 + 8],
                start=False, stop=True), inc=True)
            c0 = tc * 8
            V(lambda e, tc=tc, c0=c0, bk=bk: e.reduce_max(smc[:, 24:25], BK(bk)[:, c0:c0 + 8], axis=AX.X),
              waits=[ev_rt[tc]], drain=True)
            ev_ngm = V(lambda e: e.tensor_scalar(smc[:, 25:26], smc[:, 24:25], -1.0, None, ALU.mult),
                       inc=True, drain=True)
            evsx = Sc(lambda e, c0=c0, bk=bk: e.activation(pxp[:, c0:c0 + 8], BK(bk)[:, c0:c0 + 8],
                                                           AF.Exp, bias=smc[:, 25:26]),
                      waits=[ev_ngm], inc=True)
            bank_free[bk].append(evsx)
            V(lambda e, c0=c0: e.reduce_sum(smc[:, 26:27], pxp[:, c0:c0 + 8], axis=AX.X),
              waits=[evsx], drain=True)
            V(lambda e: e.reciprocal(smc[:, 27:28], smc[:, 26:27]), drain=True)
            V(lambda e, c0=c0: e.tensor_scalar(pxp[:, c0:c0 + 8], pxp[:, c0:c0 + 8],
                                               smc[:, 27:28], None, ALU.mult), drain=True)
            V(lambda e, c0=c0: e.reduce_max(smc[:, 24:25], pxp[:, c0:c0 + 8], axis=AX.X), drain=True)
            V(lambda e, c0=c0: e.tensor_scalar(w2k[:, c0:c0 + 8], pxp[:, c0:c0 + 8],
                                               smc[:, 24:25], None, ALU.is_ge), drain=True)
            V(lambda e, c0=c0: e.tensor_scalar(w2k[:, c0:c0 + 8], w2k[:, c0:c0 + 8],
                                               -1e30, None, ALU.mult), drain=True)
            V(lambda e, c0=c0: e.tensor_tensor(w2k[:, c0:c0 + 8], w2k[:, c0:c0 + 8],
                                               pxp[:, c0:c0 + 8], ALU.add), drain=True)
            V(lambda e, c0=c0: e.reduce_max(smc[:, 24:25], w2k[:, c0:c0 + 8], axis=AX.X), drain=True)
            V(lambda e, c0=c0: e.tensor_scalar(w2k[:, c0:c0 + 8], pxp[:, c0:c0 + 8],
                                               smc[:, 24:25], None, ALU.is_ge), drain=True)
            ev_w2k = V(lambda e, c0=c0: e.tensor_tensor(w2k[:, c0:c0 + 8], w2k[:, c0:c0 + 8],
                                                        pxp[:, c0:c0 + 8], ALU.mult),
                       inc=(tc == TC - 1), drain=True)

        wt6 = use_bank(6)
        ev_trw = None
        for tc in range(TC):
            ev_trw = T(lambda e, tc=tc: e.transpose(
                BK(6)[0:8, tc * 128:(tc + 1) * 128], w2k[:, tc * 8:tc * 8 + 8],
                ident[:, :]), waits=(wt6 + [ev_w2k]) if tc == 0 else (), inc=(tc == TC - 1))
        ev_w2kT = Sc(lambda e: e.activation(w2kT[0:8, :], BK(6)[0:8, :], AF.Identity),
                     waits=[ev_trw], inc=True)
        bank_free[6].append(ev_w2kT)

        ck("M")
        ev_hsh = {}
        for half, key in enumerate(["sw1A", "sw1B"]):
            ensure(key)
            slg, evg = piece_ev[key]
            ev_last = None
            for fl in range(8):
                hc = half * 8 + fl
                bk = hc % 4
                wt = use_bank(bk) + ([evg, ev_fin_all] if fl in (0, 1, 2, 3) else [])
                for kc in range(DC):
                    ev_last = T(lambda e, bk=bk, slg=slg, kc=kc, fl=fl: e.matmul(
                        BK(bk), WS[slg][:, kc * 1024 + fl * 128: kc * 1024 + (fl + 1) * 128],
                        Bgg[:, kc * 512:(kc + 1) * 512],
                        start=(kc == 0), stop=(kc == DC - 1)),
                        waits=wt if kc == 0 else (), inc=(kc == DC - 1))
                evs = Sc(lambda e, bk=bk, hc=hc: e.activation(
                    Bh0[:, hc * 512:(hc + 1) * 512], BK(bk), AF.Gelu,
                    bias=colsb[:, CB_SB1 + hc:CB_SB1 + hc + 1]),
                    waits=[ev_last], inc=True)
                ev_hsh[hc] = evs
                bank_free[bk].append(evs)
            consume_piece(key, ev_last)
        ck("N1")
        ev_mix = {}
        for fh in range(2):
            key = f"sw2_{fh}"
            ensure(key)
            slm, evm = piece_ev[key]
            ev_pl = None
            for tc in range(TC):
                bk = 2 + tc
                wt = use_bank(bk) + ([evm, ev_hsh[15]] if tc == 0 else [])
                for kl in range(16):
                    ev_pl = T(lambda e, bk=bk, slm=slm, kl=kl, tc=tc: e.matmul(
                        BK(bk), Bh0[:, kl * 512 + tc * 128: kl * 512 + (tc + 1) * 128],
                        WS[slm][:, kl * 512:(kl + 1) * 512],
                        start=(kl == 0), stop=False),
                        waits=wt if kl == 0 else (),
                        inc=(tc == TC - 1 and kl == 15))
                T(lambda e, bk=bk, tc=tc, fh=fh: e.matmul(
                    BK(bk), w2kT[0:8, tc * 128:(tc + 1) * 128],
                    rb2all[0:8, fh * 512:(fh + 1) * 512],
                    start=False, stop=False),
                    waits=[ev_w2kT] if (tc == 0 and fh == 0) else ())
                evb = T(lambda e, bk=bk, fh=fh: e.matmul(
                    BK(bk), ones_b[0:1, :], rowsb[0:1, 2048 + fh * 512:2048 + (fh + 1) * 512],
                    start=False, stop=True), inc=True)
                evv = V(lambda e, bk=bk, tc=tc, fh=fh: e.tensor_copy(
                    Fb[:, tc * 1024 + fh * 512: tc * 1024 + (fh + 1) * 512], BK(bk)),
                    waits=[evb], inc=True, drain=(fh == 0 and tc == 0))
                ev_mix[(tc, fh)] = evv
                bank_free[bk].append(evv)
            consume_piece(key, ev_pl)

        ck("N")
        for e_i in range(NE):
            up = (e_i % 2 == 0)
            ev_he = {}
            for half, key in enumerate([f"rw1A_{e_i}", f"rw1B_{e_i}"]):
                ensure(key)
                slg, evg = piece_ev[key]
                ev_last = None
                for fl in range(8):
                    hc = half * 8 + fl
                    bk = (0, 1, 6, 7)[hc % 4]
                    wt = use_bank(bk) + ([evg] if fl in (0, 1, 2, 3) else [])
                    for kc in range(DC):
                        ev_last = T(lambda e, bk=bk, slg=slg, kc=kc, fl=fl: e.matmul(
                            BK(bk), WS[slg][:, kc * 1024 + fl * 128: kc * 1024 + (fl + 1) * 128],
                            Bgg[:, kc * 512:(kc + 1) * 512],
                            start=(kc == 0), stop=(kc == DC - 1)),
                            waits=wt if kc == 0 else (), inc=(kc == DC - 1))
                    dap = hup(hc, 0, 512) if up else Bh0[:, hc * 512:(hc + 1) * 512]
                    evs = Sc(lambda e, bk=bk, hc=hc, dap=dap, e_i=e_i: e.activation(
                        dap, BK(bk), AF.Gelu,
                        bias=colsb[:, CB_RB1 + e_i * 16 + hc:CB_RB1 + e_i * 16 + hc + 1]),
                        waits=[ev_last], inc=True)
                    ev_he[hc] = evs
                    bank_free[bk].append(evs)
                consume_piece(key, ev_last)
            for fh in range(2):
                key = f"rw2_{fh}_{e_i}"
                ensure(key)
                slm, evm = piece_ev[key]
                ev_pl = None
                for tc in range(TC):
                    bk = 2 + tc
                    wt = use_bank(bk) + ([evm, ev_he[15]] if tc == 0 else [])
                    evb = None
                    for kl in range(16):
                        lhs = (hup(kl, tc * 128, (tc + 1) * 128) if up
                               else Bh0[:, kl * 512 + tc * 128: kl * 512 + (tc + 1) * 128])
                        evb = ev_pl = T(lambda e, bk=bk, slm=slm, kl=kl, lhs=lhs: e.matmul(
                            BK(bk), lhs,
                            WS[slm][:, kl * 512:(kl + 1) * 512],
                            start=(kl == 0), stop=(kl == 15)),
                            waits=wt if kl == 0 else (),
                            inc=(kl == 15))
                    mixw = [evb, ev_w2k] + ([ev_mix[(tc, fh)]] if e_i == 0 else [])
                    evv = V(lambda e, bk=bk, tc=tc, fh=fh, e_i=e_i: e.scalar_tensor_tensor(
                        Fb[:, tc * 1024 + fh * 512: tc * 1024 + (fh + 1) * 512], BK(bk),
                        w2k[:, tc * 8 + e_i:tc * 8 + e_i + 1],
                        Fb[:, tc * 1024 + fh * 512: tc * 1024 + (fh + 1) * 512],
                        ALU.mult, ALU.add),
                        waits=mixw, inc=True, drain=(tc == 0 and fh == 0))
                    ev_mix[(tc, fh)] = evv
                    bank_free[bk].append(evv)
                    if e_i == NE - 1:
                        V(lambda e, tc=tc, fh=fh: e.bn_stats(
                            sttc[:, tc * 12 + 6 * fh: tc * 12 + 6 * fh + 6],
                            Fb[:, tc * 1024 + 512 * fh: tc * 1024 + 512 * (fh + 1)]), drain=True)
                        if fh == 1:
                            V(lambda e, tc=tc: e.bn_aggr(st2c[:, tc * 2:tc * 2 + 2],
                                                         sttc[:, tc * 12:tc * 12 + 12]), drain=True)
                consume_piece(key, ev_pl)

        ck("O")
        ev_st = V(lambda e: e.engine_nop(), inc=True)
        Sc(lambda e: e.activation(smc[:, 4:8], st2c[:, 1:8:2], AF.Ln, bias=epsc[:, 0:1]),
           waits=[ev_st])
        ev_r5 = Sc(lambda e: e.activation(smc[:, 8:12], smc[:, 4:8], AF.Exp, scale=-0.5),
                   inc=True, drain=True)
        ev_out = None
        for tc in range(TC):
            V(lambda e, tc=tc: e.tensor_scalar(
                Fc[:, tc * 1024:(tc + 1) * 1024], Fb[:, tc * 1024:(tc + 1) * 1024],
                st2c[:, 2 * tc:2 * tc + 1], smc[:, 8 + tc:9 + tc],
                ALU.subtract, ALU.mult), waits=[ev_r5] if tc == 0 else (), drain=(tc == 0))
        for tc in range(TC):
            ev_out = V(lambda e, tc=tc: e.tensor_tensor(
                Fc[:, tc * 1024:(tc + 1) * 1024], Fc[:, tc * 1024:(tc + 1) * 1024],
                Fa[:, tc * 1024:(tc + 1) * 1024], ALU.add), inc=(tc == TC - 1),
                drain=(tc == 0))
        ev_store = G(lambda e, tt=t: e.dma_start(
            dy.ap()[tt * TT:(tt + 1) * TT, :].rearrange("(tc p) f -> p tc f", p=128),
            Fc[:, :].rearrange("p (tc f) -> p tc f", f=D)),
            waits=[ev_out], qsem="qst")
        ev_fa_free = ev_out
        if t + 1 < NT:
            AB_stash = emit_AB(ev_tnorm_next, first=False)

    except _StopEmit:
        pass

    if CNT["qst"] == 0:
        ev_dummy = G(lambda e: e.dma_start(
            dy.ap()[0:TT, :].rearrange("(tc p) f -> p tc f", p=128),
            Fx0[:, :].rearrange("p (tc f) -> p tc f", f=D)), qsem="qst")
    G(lambda e: e.wait_ge(sem["qst"], CNT["qst"]))

    block = ctx.enter_context(nc.Block())

    @block.sync
    def _(q):
        for f in prog["Q"]:
            f(q)

    @block.gpsimd
    def _(g):
        for f in prog["G"]:
            f(g)

    @block.tensor
    def _(tt):
        for f in prog["T"]:
            f(tt)

    @block.vector
    def _(v):
        for f in prog["V"]:
            f(v)

    @block.scalar
    def _(s):
        for f in prog["S"]:
            f(s)

    ctx.close()
    return nc


def make_inputs(full, b, S):
    f = full
    im = {
        "x": np.ascontiguousarray(f["L0_attn_x"][b, :S, :]),
        "gate_w": f["L0_attn_gate_w"], "state_w": f["L0_attn_state_w"],
        "out_w": f["L0_attn_out_w"], "mlp_w1": f["L0_attn_mlp_w1"],
        "mlp_w2": f["L0_attn_mlp_w2"], "sw1": f["L0_ffn_sw1"], "sw2": f["L0_ffn_sw2"],
        "router_w": f["L0_ffn_router_w"], "rw1": f["L0_ffn_rw1"], "rw2": f["L0_ffn_rw2"],
        "norm1_w": f["L0_attn_norm1_w"].reshape(1, D), "norm1_b": f["L0_attn_norm1_b"].reshape(1, D),
        "norm2_w": f["L0_attn_norm2_w"].reshape(1, D), "norm2_b": f["L0_attn_norm2_b"].reshape(1, D),
        "gate_b": f["L0_attn_gate_b"].reshape(1, 2 * D), "state_b": f["L0_attn_state_b"].reshape(1, D),
        "out_b": f["L0_attn_out_b"].reshape(1, D), "mlp_b1": f["L0_attn_mlp_b1"].reshape(1, HM),
        "mlp_b2": f["L0_attn_mlp_b2"].reshape(1, D), "router_b": f["L0_ffn_router_b"].reshape(1, NE),
        "sb1": f["L0_ffn_sb1"].reshape(1, HF), "sb2": f["L0_ffn_sb2"].reshape(1, D),
        "rb1": f["L0_ffn_rb1"], "rb2": f["L0_ffn_rb2"].reshape(1, NE * D),
        "state_b_c": f["L0_attn_state_b"].reshape(8, 128).T,
        "gate_b_c": f["L0_attn_gate_b"].reshape(16, 128).T,
        "mlp_b1_c": f["L0_attn_mlp_b1"].reshape(32, 128).T,
        "sb1_c": f["L0_ffn_sb1"].reshape(16, 128).T,
        "rb1_c": f["L0_ffn_rb1"].reshape(NE, 16, 128).transpose(2, 0, 1).reshape(128, 128),
        "n1w_c": f["L0_attn_norm1_w"].reshape(8, 128).T,
        "n1b_c": f["L0_attn_norm1_b"].reshape(8, 128).T,
        "n2w_c": f["L0_attn_norm2_w"].reshape(8, 128).T,
        "n2b_c": f["L0_attn_norm2_b"].reshape(8, 128).T,
        "ident_f32": np.eye(128, dtype=np.float32),
        "ones_f": np.ones((1, 128), np.float32),
    }
    import ml_dtypes
    im["ones_b"] = np.ones((1, 128), ml_dtypes.bfloat16)
    im = {k: np.ascontiguousarray(v) for k, v in im.items()}
    return im


def kernel(**inputs):
    S = inputs["L0_attn_x"].shape[1]
    B = inputs["L0_attn_x"].shape[0]
    nc = build(S)
    in_maps = [make_inputs(inputs, b, S) for b in range(B)]
    res = run_bass_kernel_spmd(nc, in_maps, core_ids=list(range(B)))
    out = np.stack([res.results[b]["y"] for b in range(B)], axis=0)
    return out.astype(np.float32)
